# version 01
# baseline (speedup 1.0000x reference)
import os
import numpy as np
import ml_dtypes

import concourse.bass as bass
import concourse.mybir as mybir
from concourse.bass_utils import run_bass_kernel_spmd

F32 = mybir.dt.float32
BF16 = mybir.dt.bfloat16
AF = mybir.ActivationFunctionType
ALU = mybir.AluOpType
AX = mybir.AxisListType
bfdt = ml_dtypes.bfloat16

S = 4096
D = 1024
DC = 8
NTILE = 32
NSLAB = 8
QBS = 256
NQB = S // QBS
KC = 32
NE = 10
HEXP = 768
HC1 = 12
HC2 = 6
NHALF = 2048
TTH = 16

BASE = 16512
HCORD = [0, 6, 1, 7, 2, 8, 3, 9, 4, 10, 5, 11]


def build(phases=4, debug=False):
    nc = bass.Bass("TRN2", target_bir_lowering=False)

    x_d = nc.dram_tensor("x", [S, D], F32, kind="ExternalInput").ap()
    wqkv_d = nc.dram_tensor("wqkv", [D, 3 * D], F32, kind="ExternalInput").ap()
    bqk_d = nc.dram_tensor("bqk", [128, 16], F32, kind="ExternalInput").ap()
    bv_d = nc.dram_tensor("bv", [1, D], F32, kind="ExternalInput").ap()
    bo_d = nc.dram_tensor("bo", [1, D], F32, kind="ExternalInput").ap()
    wout_d = nc.dram_tensor("wout", [D, D], F32, kind="ExternalInput").ap()
    gwt_d = nc.dram_tensor("gwt", [D, 8], F32, kind="ExternalInput").ap()
    w1a_d = nc.dram_tensor("w1a", [NE, D, 2 * HEXP], F32, kind="ExternalInput").ap()
    w2a_d = nc.dram_tensor("w2a", [NE, HEXP, D], F32, kind="ExternalInput").ap()
    out_d = nc.dram_tensor("out", [S, D], F32, kind="ExternalOutput").ap()

    dbg_kind = "ExternalOutput" if debug else "Internal"
    qh = nc.dram_tensor("q_hbm", [DC, 128, S], BF16, kind=dbg_kind).ap()
    vh = nc.dram_tensor("v_hbm", [S, D], BF16, kind=dbg_kind).ap()
    ah = nc.dram_tensor("attn_hbm", [S, D], BF16, kind=dbg_kind).ap()
    xfh = nc.dram_tensor("xnf_hbm", [DC, 128, S], BF16, kind=dbg_kind).ap()
    if debug:
        dbgw_d = nc.dram_tensor("dbg_w", [128, NTILE * 8], F32, kind="ExternalOutput").ap()

    idb = nc.inline_tensor(np.eye(128, dtype=bfdt), name="ident_bf").ap()
    i8f = nc.inline_tensor(np.eye(8, dtype=np.float32), name="ident8_f").ap()
    onesc = nc.inline_tensor(np.ones((128, 1), dtype=np.float32), name="ones_col").ap()
    onesr = nc.inline_tensor(np.ones((1, 128), dtype=np.float32), name="ones_row").ap()

    def at(name, shape, dtype, off):
        assert off >= BASE, name
        nbytes = int(np.prod(shape[1:])) * (2 if dtype == BF16 else 4)
        assert off + nbytes <= 229344, (name, off, nbytes)
        return nc.alloc_sbuf_tensor_at(name, shape, dtype, offset=off)

    A = BASE
    B = BASE + 65536
    C = BASE + 131072
    DD = BASE + 180224

    xnf = at("xnf", [128, DC * S], BF16, A)
    v_sb = at("v_sb", [128, NTILE * D], BF16, A)
    z_sb = at("z_sb", [128, TTH * D], F32, A)

    x_buf = [at(f"x_buf{i}", [128, 4 * D], F32, B + i * 16384) for i in range(2)]
    xn32 = [at(f"xn32_{i}", [128, D], F32, B + 32768 + i * 4096) for i in range(2)]
    hi_tok = [at(f"hi{i}", [128, D], BF16, B + 40960 + i * 2048) for i in range(2)]
    lo_tok = [at(f"lo{i}", [128, D], BF16, B + 45056 + i * 2048) for i in range(2)]
    lo_f = at("lof", [128, DC * 512], BF16, B + 49152)
    k_sb = at("k_sb", [128, DC * S], BF16, B)
    y_sb = at("y_sb", [128, HC2 * NHALF], BF16, B)
    sg_sb = at("sg_sb", [128, HC2 * NHALF], BF16, B + 24576)
    w2_buf = at("w2_buf", [128, HC2 * D], BF16, B + 49152)
    brow = at("brow", [1, D], F32, B + 57344)
    lgT = at("lgT", [8, 512], F32, B + 61440)
    junk = at("junk", [128, D], BF16, B + 63488)

    wqkv_bf = at("wqkv_bf", [128, DC * 3 * D], BF16, C)
    wout_bf = at("wout_bf", [128, DC * D], BF16, C)
    qf_sb = at("qf_sb", [128, DC * QBS], BF16, C + 16384)
    probs = at("probs", [128, KC * QBS], BF16, C + 20480)
    spp = at("spp", [128, QBS], F32, C + 36864)
    ctx_sb = at("ctx_sb", [128, DC * QBS], BF16, C + 37888)
    attn_st = at("attn_st", [128, 2 * D], BF16, C + 41984)
    xh_sb = at("xh_sb", [128, DC * NHALF], BF16, C)
    x_re = [at(f"x_re{i}", [128, D], F32, C + 32768 + i * 4096) for i in range(2)]
    attn_re = [at(f"attn_re{i}", [128, D], BF16, C + 40960 + i * 2048) for i in range(2)]
    out_st = at("out_st", [128, D], F32, C + 45056)

    w1_buf = at("w1_buf", [128, DC * 2 * HEXP], BF16, DD)
    qstage = [at(f"qstage{i}", [128, 2048], BF16, DD + i * 4096) for i in range(2)]
    vstage = [at(f"vstage{i}", [128, D], BF16, DD + 8192 + i * 2048) for i in range(2)]
    o = DD + 24576
    w_all = at("w_all", [128, NTILE * 8], F32, o); o += 1024
    gwt_f = at("gwt_f", [128, DC * 8], F32, o); o += 256
    gw_hi = at("gw_hi", [128, DC * 8], BF16, o); o += 128
    gw_lo = at("gw_lo", [128, DC * 8], BF16, o); o += 128
    b_sb = at("b_sb", [128, 16], F32, o); o += 64
    id_sb = at("id_sb", [128, 128], BF16, o); o += 256
    i8_sb = at("i8_sb", [8, 8], F32, o); o += 32
    onec_sb = at("onec_sb", [128, 1], F32, o); o += 32
    oner_sb = at("oner_sb", [1, 128], F32, o); o += 512
    lg_tok = at("lg_tok", [128, NTILE * 8], F32, o); o += 1024
    negm8 = at("negm8", [128, NTILE], F32, o); o += 128
    stt = at("stt", [128, 16], F32, o); o += 64
    bv_bc = at("bv_bc", [128, D], BF16, o); o += 2048
    bo_bc = at("bo_bc", [128, D], BF16, o); o += 2048
    s_inv = at("s_inv", [128, 4], F32, o); o += 32
    prob8 = at("prob8", [128, 8], F32, o); o += 32
    msk8 = at("msk8", [128, 8], F32, o); o += 32
    l2t = at("l2t", [128, 8], F32, o); o += 32
    assert o <= 229344, o

    PB = [nc.alloc_psum_tensor(f"pb{i}", [128, 512], F32) for i in range(8)]

    names = [
        "s_cst", "s_x1", "s_wq", "s_xnfo", "s_qo", "s_vo", "s_vi", "s_qf", "s_ao",
        "s_wo", "s_w1", "s_w2", "s_xh", "s_x2", "s_ai", "s_out", "s_bvbo",
        "s_v1", "s_vu", "s_c1", "s_vhi", "s_th", "s_tl", "s_v1e", "s_vloe",
        "s_tg", "s_vlg", "s_tk", "s_vtok", "s_tb", "s_vgw", "s_gsm",
        "s_ev2", "s_t2",
        "s_sc", "s_exp", "s_spp", "s_cx", "s_cve", "s_ts3", "s_siv", "s_to", "s_ave",
        "s_m1g", "s_ye", "s_ge", "s_m1f", "s_act", "s_m2t", "s_m2f", "s_zadd", "s_cmb",
    ]
    sm = type("SM", (), {n: nc.alloc_semaphore(n) for n in names})


    def emit_dbg():
        if not debug:
            return
        with nc.Block("dbg") as Bl:
            @Bl.sync
            def _(sy):
                sy.dma_start(dbgw_d, w_all[:]).then_inc(sm.s_cst, 16)
                sy.wait_ge(sm.s_cst, 16 * 7)

    def xnf3(apx):
        return apx.rearrange("p (dc t) -> p dc t", dc=DC)

    with nc.Block("ph1") as Bl:

        @Bl.sync
        def _(sy):
            sy.dma_start(id_sb[:], idb).then_inc(sm.s_cst, 16)
            sy.dma_start(i8_sb[:], i8f).then_inc(sm.s_cst, 16)
            sy.dma_start(onec_sb[:], onesc).then_inc(sm.s_cst, 16)
            sy.dma_start(oner_sb[:], onesr).then_inc(sm.s_cst, 16)
            sy.dma_start(gwt_f[:].rearrange("p (dc e) -> p dc e", dc=DC),
                         gwt_d.rearrange("(dc p) e -> p dc e", p=128)).then_inc(sm.s_cst, 16)
            sy.dma_start(b_sb[:], bqk_d).then_inc(sm.s_cst, 16)
            xs = x_d.rearrange("(s q p) d -> s p q d", p=128, q=4)
            for s in range(NSLAB):
                if s >= 2:
                    sy.wait_ge(sm.s_v1, 4 * s - 4)
                sy.dma_start(x_buf[s % 2][:].rearrange("p (q d) -> p q d", q=4),
                             xs[s]).then_inc(sm.s_x1, 16)
            xo = xfh.rearrange("dc p (sl t) -> sl p dc t", sl=NSLAB)
            for s in range(NSLAB):
                sy.wait_ge(sm.s_v1e, 4 * s + 4)
                sy.dma_start(xo[s],
                             xnf3(xnf[:])[:, :, s * 512:(s + 1) * 512]).then_inc(sm.s_xnfo, 16)
            sy.dma_start(brow[:], bv_d).then_inc(sm.s_bvbo, 16)
            sy.wait_ge(sm.s_tb, 1)
            sy.dma_start(brow[:], bo_d).then_inc(sm.s_bvbo, 16)

        @Bl.gpsimd
        def _(g):
            g.dma_start(wqkv_bf[:].rearrange("p (dc j) -> p dc j", dc=DC),
                        wqkv_d.rearrange("(dc p) j -> p dc j", p=128)).then_inc(sm.s_wq, 16)

        @Bl.vector
        def _(v):
            v.wait_ge(sm.s_cst, 16 * 6)
            v.tensor_copy(gw_hi[:], gwt_f[:])
            v.tensor_sub(gw_lo[:], gwt_f[:], gw_hi[:]).then_inc(sm.s_vgw, 1)
            for ti in range(NTILE):
                s, q = divmod(ti, 4)
                p = ti % 2
                xcol = x_buf[s % 2][:, q * D:(q + 1) * D]
                c0 = p * 8
                v.wait_ge(sm.s_x1, 16 * (s + 1))
                if ti >= 2:
                    v.wait_ge(sm.s_c1, 2 * ti - 2)
                v.reduce_sum(stt[:, c0:c0 + 1], xcol, AX.X)
                v.scalar_tensor_tensor(junk[:], xcol, 1.0, xcol, ALU.mult, ALU.mult,
                                       accum_out=stt[:, c0 + 1:c0 + 2])
                v.drain()
                v.tensor_scalar(stt[:, c0 + 2:c0 + 3], stt[:, c0:c0 + 1], 1.0 / D, None, ALU.mult)
                v.tensor_scalar(stt[:, c0 + 3:c0 + 4], stt[:, c0 + 1:c0 + 2], 1.0 / D, 1e-5,
                                ALU.mult, ALU.add)
                v.drain()
                v.tensor_mul(stt[:, c0 + 4:c0 + 5], stt[:, c0 + 2:c0 + 3], stt[:, c0 + 2:c0 + 3])
                v.drain()
                v.scalar_tensor_tensor(stt[:, c0 + 3:c0 + 4], stt[:, c0 + 4:c0 + 5], -1.0,
                                       stt[:, c0 + 3:c0 + 4], ALU.mult, ALU.add)
                v.drain()
                v.reciprocal(stt[:, c0 + 4:c0 + 5], stt[:, c0 + 3:c0 + 4])
                v.drain().then_inc(sm.s_vu, 1)
                v.wait_ge(sm.s_c1, 2 * ti + 1)
                v.tensor_mul(stt[:, c0 + 6:c0 + 7], stt[:, c0 + 5:c0 + 6], stt[:, c0 + 5:c0 + 6])
                v.drain()
                v.tensor_mul(stt[:, c0 + 6:c0 + 7], stt[:, c0 + 6:c0 + 7], stt[:, c0 + 3:c0 + 4])
                v.drain()
                v.tensor_scalar(stt[:, c0 + 6:c0 + 7], stt[:, c0 + 6:c0 + 7], -0.5, 1.5, ALU.mult, ALU.add)
                v.drain()
                v.tensor_mul(stt[:, c0 + 6:c0 + 7], stt[:, c0 + 6:c0 + 7], stt[:, c0 + 5:c0 + 6])
                v.drain()
                v.tensor_scalar(xn32[p][:], xcol, stt[:, c0 + 2:c0 + 3], stt[:, c0 + 6:c0 + 7],
                                ALU.subtract, ALU.mult).then_inc(sm.s_v1, 1)
                v.wait_ge(sm.s_c1, 2 * ti + 2)
                if ti >= 2:
                    v.wait_ge(sm.s_tl, ti - 1)
                v.tensor_sub(lo_tok[p][:], xn32[p][:], hi_tok[p][:]).then_inc(sm.s_vhi, 1)
                v.wait_ge(sm.s_th, ti + 1)
                pb0 = PB[p * 2]
                pb1 = PB[p * 2 + 1]
                dst = xnf3(xnf[:])[:, :, ti * 128:(ti + 1) * 128]
                v.tensor_copy(dst[:, 0:4, :], pb0[:].rearrange("p (a t) -> p a t", a=4))
                v.tensor_copy(dst[:, 4:8, :],
                              pb1[:].rearrange("p (a t) -> p a t", a=4)).then_inc(sm.s_v1e, 1)
                if q == 3:
                    v.wait_ge(sm.s_tg, s + 1)
                    v.tensor_copy(lgT[:], PB[0][0:8, :]).then_inc(sm.s_vlg, 1)
                    v.wait_ge(sm.s_tk, s + 1)
                    v.tensor_copy(lg_tok[:, s * 32:(s + 1) * 32], PB[1][:, 0:32])
                    v.drain().then_inc(sm.s_vtok, 1)
                    v.drain()
                    for qq in range(4):
                        tj = s * 4 + qq
                        lcol = lg_tok[:, tj * 8:(tj + 1) * 8]
                        v.reduce_max(negm8[:, tj:tj + 1], lcol, AX.X)
                        v.drain()
                        v.tensor_scalar(negm8[:, tj:tj + 1], negm8[:, tj:tj + 1], -1.0, None, ALU.mult)
                        v.drain()
            v.wait_ge(sm.s_tb, 1)
            v.tensor_copy(bv_bc[:, 0:512], PB[0][:])
            v.tensor_copy(bv_bc[:, 512:1024], PB[1][:])
            v.wait_ge(sm.s_tb, 4)
            v.tensor_copy(bo_bc[:, 0:512], PB[2][:])
            v.tensor_copy(bo_bc[:, 512:1024], PB[3][:])
            for ti in range(NTILE):
                v.wait_ge(sm.s_gsm, ti + 1)
                lcol = lg_tok[:, ti * 8:(ti + 1) * 8]
                v.reduce_sum(stt[:, 0:1], prob8[:], AX.X)
                v.drain()
                v.reciprocal(stt[:, 1:2], stt[:, 0:1])
                v.reduce_max(stt[:, 2:3], lcol, AX.X)
                v.drain()
                v.tensor_scalar(msk8[:], lcol, stt[:, 2:3], None, ALU.is_ge)
                v.drain()
                v.scalar_tensor_tensor(l2t[:], msk8[:], -1e30, lcol, ALU.mult, ALU.add)
                v.drain()
                v.reduce_max(stt[:, 3:4], l2t[:], AX.X)
                v.drain()
                v.tensor_scalar(msk8[:], lcol, stt[:, 3:4], None, ALU.is_ge)
                v.drain()
                v.tensor_mul(msk8[:], msk8[:], prob8[:])
                v.drain()
                v.tensor_scalar(w_all[:, ti * 8:(ti + 1) * 8], msk8[:], stt[:, 1:2], None,
                                ALU.mult)
                v.drain().then_inc(sm.s_v1, 1)

        @Bl.scalar
        def _(sc):
            for ti in range(NTILE):
                p = ti % 2
                c0 = p * 8
                sc.wait_ge(sm.s_vu, ti + 1)
                sc.sqrt(stt[:, c0 + 5:c0 + 6], stt[:, c0 + 4:c0 + 5])
                sc.drain().then_inc(sm.s_c1, 1)
                sc.wait_ge(sm.s_v1, ti + 1)
                sc.copy(hi_tok[p][:], xn32[p][:]).then_inc(sm.s_c1, 1)
                sc.wait_ge(sm.s_tl, ti + 1)
                if ti >= 4:
                    sc.wait_ge(sm.s_tg, ti // 4)
                q = ti % 4
                dst = lo_f[:].rearrange("p (dc t) -> p dc t", dc=DC)[:, :, q * 128:(q + 1) * 128]
                sc.copy(dst[:, 0:4, :], PB[4 + p * 2][:].rearrange("p (a t) -> p a t", a=4))
                sc.copy(dst[:, 4:8, :],
                        PB[4 + p * 2 + 1][:].rearrange("p (a t) -> p a t", a=4)).then_inc(sm.s_vloe, 1)
            for ti in range(NTILE):
                sc.wait_ge(sm.s_vtok, ti // 4 + 1)
                if ti >= 1:
                    sc.wait_ge(sm.s_v1, NTILE + ti)
                sc.activation(prob8[:], lg_tok[:, ti * 8:(ti + 1) * 8], AF.Exp,
                              bias=negm8[:, ti:ti + 1])
                sc.drain().then_inc(sm.s_gsm, 1)

        @Bl.tensor
        def _(t):
            t.wait_ge(sm.s_cst, 16 * 6)
            for ti in range(NTILE):
                p = ti % 2
                s, q = divmod(ti, 4)
                t.wait_ge(sm.s_c1, 2 * ti + 2)
                if ti >= 2:
                    t.wait_ge(sm.s_v1e, ti - 1)
                if q == 0 and s >= 1 and p == 0:
                    t.wait_ge(sm.s_vlg, s)
                    t.wait_ge(sm.s_vtok, s)
                for dc in range(DC):
                    lhs = hi_tok[p][:, dc * 128:(dc + 1) * 128]
                    t.ldweights(lhs)
                    ins = t.matmul(PB[p * 2 + dc // 4][:, (dc % 4) * 128:(dc % 4 + 1) * 128],
                                   lhs, id_sb[:], start=True, stop=True)
                ins.then_inc(sm.s_th, 1)
                t.wait_ge(sm.s_vhi, ti + 1)
                if ti >= 2:
                    t.wait_ge(sm.s_vloe, ti - 1)
                for dc in range(DC):
                    lhs = lo_tok[p][:, dc * 128:(dc + 1) * 128]
                    t.ldweights(lhs)
                    ins = t.matmul(PB[4 + p * 2 + dc // 4][:, (dc % 4) * 128:(dc % 4 + 1) * 128],
                                   lhs, id_sb[:], start=True, stop=True)
                ins.then_inc(sm.s_tl, 1)
                if q == 3:
                    t.wait_ge(sm.s_vgw, 1)
                    t.wait_ge(sm.s_v1e, 4 * s + 4)
                    t.wait_ge(sm.s_vloe, 4 * s + 4)
                    if s >= 1:
                        t.wait_ge(sm.s_vlg, s)
                    nmm = 0
                    for (gw, mvsrc) in ((gw_hi, "hi"), (gw_lo, "hi"), (gw_hi, "lo")):
                        for dc in range(DC):
                            t.ldweights(gw[:, dc * 8:(dc + 1) * 8])
                            if mvsrc == "hi":
                                mv = xnf3(xnf[:])[:, dc, s * 512:(s + 1) * 512]
                            else:
                                mv = lo_f[:, dc * 512:(dc + 1) * 512]
                            ins = t.matmul(PB[0][0:8, :], gw[:, dc * 8:(dc + 1) * 8], mv,
                                           start=(nmm == 0), stop=(nmm == 23))
                            nmm += 1
                    ins.then_inc(sm.s_tg, 1)
                    t.wait_ge(sm.s_vlg, s + 1)
                    if s >= 1:
                        t.wait_ge(sm.s_vtok, s)
                    for qq in range(4):
                        ins = t.matmul(PB[1][:, qq * 8:(qq + 1) * 8],
                                       lgT[0:8, qq * 128:(qq + 1) * 128], i8_sb[:],
                                       start=True, stop=True)
                    ins.then_inc(sm.s_tk, 1)
            t.wait_ge(sm.s_bvbo, 16)
            t.wait_ge(sm.s_v1e, NTILE)
            t.wait_ge(sm.s_vlg, NSLAB)
            t.wait_ge(sm.s_vtok, NSLAB)
            t.matmul(PB[0][:], oner_sb[:], brow[:, 0:512], start=True, stop=True)
            t.matmul(PB[1][:], oner_sb[:], brow[:, 512:1024], start=True, stop=True).then_inc(sm.s_tb, 1)
            t.wait_ge(sm.s_bvbo, 32)
            t.wait_ge(sm.s_vloe, NTILE)
            t.matmul(PB[2][:], oner_sb[:], brow[:, 0:512], start=True, stop=True)
            t.matmul(PB[3][:], oner_sb[:], brow[:, 512:1024], start=True, stop=True).then_inc(sm.s_tb, 3)

    if phases < 2:
        emit_dbg()
        return nc

    with nc.Block("ph2") as Bl:

        @Bl.tensor
        def _(t):
            t.wait_ge(sm.s_wq, 16)
            for u in range(32):
                oc, tbg = divmod(u, 2)
                quad = u % 2
                if u >= 2:
                    t.wait_ge(sm.s_ev2, 4 * (u - 1))
                for dc in range(DC):
                    lhs = wqkv_bf[:, dc * 3072 + oc * 128: dc * 3072 + (oc + 1) * 128]
                    t.ldweights(lhs)
                    for tb in range(4):
                        ins = t.matmul(PB[quad * 4 + tb][:], lhs,
                                       xnf3(xnf[:])[:, dc, (tbg * 4 + tb) * 512:(tbg * 4 + tb + 1) * 512],
                                       start=(dc == 0), stop=(dc == DC - 1))
                ins.then_inc(sm.s_t2, 1)
            for tt in range(NTILE):
                pr = tt % 2
                if tt < 2:
                    t.wait_ge(sm.s_ev2, 124)
                else:
                    t.wait_ge(sm.s_ev2, 128 + 2 * (tt - 1))
                for dc in range(DC):
                    lhs = xnf3(xnf[:])[:, dc, tt * 128:(tt + 1) * 128]
                    t.ldweights(lhs)
                    for h in range(2):
                        ins = t.matmul(PB[pr * 2 + h][:], lhs,
                                       wqkv_bf[:, dc * 3072 + 2048 + h * 512: dc * 3072 + 2048 + (h + 1) * 512],
                                       start=(dc == 0), stop=(dc == DC - 1))
                ins.then_inc(sm.s_t2, 1)

        @Bl.scalar
        def _(sc):
            for u in range(32):
                oc, tbg = divmod(u, 2)
                quad = u % 2
                sc.wait_ge(sm.s_t2, u + 1)
                if oc < 8 and u >= 2:
                    sc.wait_ge(sm.s_qo, 16 * (u - 1))
                for tb in range(4):
                    tok = (tbg * 4 + tb) * 512
                    if oc < 8:
                        dst = qstage[u % 2][:, tb * 512:(tb + 1) * 512]
                    else:
                        dst = k_sb[:, (oc - 8) * 4096 + tok:(oc - 8) * 4096 + tok + 512]
                    ins = sc.activation(dst, PB[quad * 4 + tb][:], AF.Identity,
                                        bias=b_sb[:, oc:oc + 1])
                ins.then_inc(sm.s_ev2, 4)

        @Bl.vector
        def _(v):
            for tt in range(NTILE):
                pr = tt % 2
                v.wait_ge(sm.s_t2, 32 + tt + 1)
                if tt >= 2:
                    v.wait_ge(sm.s_vo, 16 * (tt - 1))
                for h in range(2):
                    ins = v.tensor_tensor(vstage[pr][:, h * 512:(h + 1) * 512],
                                          PB[pr * 2 + h][:], bv_bc[:, h * 512:(h + 1) * 512], ALU.add)
                ins.then_inc(sm.s_ev2, 2)

        @Bl.sync
        def _(sy):
            qo = qh.rearrange("dc p (g t) -> dc g p t", g=2)
            for u in range(16):
                sy.wait_ge(sm.s_ev2, 4 * (u + 1))
                sy.dma_start(qo[u // 2, u % 2], qstage[u % 2][:]).then_inc(sm.s_qo, 16)
            vo = vh.rearrange("(tt p) d -> tt p d", p=128)
            for tt in range(NTILE):
                pr = tt % 2
                sy.wait_ge(sm.s_ev2, 128 + 2 * (tt + 1))
                sy.dma_start(vo[tt], vstage[pr][:]).then_inc(sm.s_vo, 16)

    if phases < 3:
        emit_dbg()
        return nc

    PH3_LVL = int(os.environ.get("PH3_LVL", "4"))
    PH3_NOEVAC = bool(int(os.environ.get("PH3_NOEVAC", "0")))
    with nc.Block("ph3") as Bl:

        @Bl.gpsimd
        def _(g):
            g.wait_ge(sm.s_qo, 16 * 16)
            g.wait_ge(sm.s_vo, 16 * 32)
            g.dma_start(wout_bf[:].rearrange("p (dc j) -> p dc j", dc=DC),
                        wout_d.rearrange("(dc p) j -> p dc j", p=128)).then_inc(sm.s_wo, 16)
            g.dma_start(w1_buf[:].rearrange("p (dc j) -> p dc j", dc=DC),
                        w1a_d.rearrange("e (dc p) j -> e p dc j", p=128)[0]).then_inc(sm.s_w1, 16)

        @Bl.sync
        def _(sy):
            sy.wait_ge(sm.s_xnfo, 16 * NSLAB)
            vi = vh.rearrange("(g tt p) d -> g p tt d", p=128, tt=8)
            vs = v_sb[:].rearrange("p (g tt d) -> p g tt d", g=4, tt=8)
            for gg in range(4):
                sy.dma_start(vs[:, gg], vi[gg]).then_inc(sm.s_vi, 16)
            qi = qh.rearrange("dc p (qb t) -> qb p dc t", qb=NQB)
            ao = ah.rearrange("(qb qq p) d -> qb p qq d", p=128, qq=2)
            store_on = PH3_LVL >= 4 and not PH3_NOEVAC

            def qf_load(qb):
                if qb == 0:
                    sy.wait_ge(sm.s_qo, 16 * 16)
                else:
                    sy.wait_ge(sm.s_sc, 32 * qb)
                sy.dma_start(qf_sb[:].rearrange("p (dc t) -> p dc t", dc=DC),
                             qi[qb]).then_inc(sm.s_qf, 16)

            qf_load(0)
            for qb in range(NQB):
                if qb + 1 < NQB:
                    qf_load(qb + 1)
                if store_on:
                    sy.wait_ge(sm.s_ave, 4 * (qb + 1))
                    sy.dma_start(ao[qb],
                                 attn_st[:].rearrange("p (qq d) -> p qq d", qq=2)).then_inc(sm.s_ao, 16)

        @Bl.tensor
        def _(t):
            for qb in range(NQB):
                t.wait_ge(sm.s_qf, 16 * (qb + 1))
                for kc in range(KC):
                    gi = qb * KC + kc
                    if gi >= 2:
                        t.wait_ge(sm.s_exp, gi - 1)
                    slot = kc % 4
                    pbs = PB[slot % 2][:, (slot // 2) * 256:(slot // 2 + 1) * 256]
                    for dc in range(DC):
                        lhs = k_sb[:, dc * 4096 + kc * 128: dc * 4096 + (kc + 1) * 128]
                        t.ldweights(lhs)
                        ins = t.matmul(pbs, lhs, qf_sb[:, dc * QBS:(dc + 1) * QBS],
                                       start=(dc == 0), stop=(dc == DC - 1))
                    ins.then_inc(sm.s_sc, 1)
                if PH3_LVL < 2:
                    continue
                t.wait_ge(sm.s_exp, 32 * (qb + 1))
                if qb == 0:
                    t.wait_ge(sm.s_vi, 64)
                elif PH3_LVL >= 4 and not PH3_NOEVAC:
                    t.wait_ge(sm.s_ave, 4 * qb)
                else:
                    t.wait_ge(sm.s_cve, qb)
                for dc in range(DC):
                    pbc = PB[2 + dc // 2][:, (dc % 2) * 256:(dc % 2 + 1) * 256]
                    for kc in range(KC):
                        lhs = v_sb[:, kc * 1024 + dc * 128: kc * 1024 + (dc + 1) * 128]
                        t.ldweights(lhs)
                        ins = t.matmul(pbc, lhs, probs[:, kc * QBS:(kc + 1) * QBS],
                                       start=(kc == 0), stop=(kc == KC - 1))
                    ins.then_inc(sm.s_cx, 1)
                if PH3_LVL < 3:
                    continue
                t.wait_ge(sm.s_spp, 32 * (qb + 1))
                if qb >= 1:
                    t.wait_ge(sm.s_siv, qb)
                for qq in range(2):
                    ins = t.matmul(PB[6][:, qq:qq + 1], spp[:, qq * 128:(qq + 1) * 128],
                                   onec_sb[:], start=True, stop=True)
                ins.then_inc(sm.s_ts3, 1)
                if PH3_LVL < 4:
                    continue
                t.wait_ge(sm.s_cve, qb + 1)
                if qb == 0:
                    t.wait_ge(sm.s_wo, 16)
                for qq in range(2):
                    for dc in range(DC):
                        lhs = ctx_sb[:, dc * 256 + qq * 128: dc * 256 + (qq + 1) * 128]
                        t.ldweights(lhs)
                        for h in range(2):
                            ins = t.matmul(PB[2 + qq * 2 + h][:], lhs,
                                           wout_bf[:, dc * 1024 + h * 512: dc * 1024 + (h + 1) * 512],
                                           start=(dc == 0), stop=(dc == DC - 1))
                    ins.then_inc(sm.s_to, 1)

        @Bl.scalar
        def _(sc):
            for qb in range(NQB):
                for kc in range(KC):
                    gi = qb * KC + kc
                    sc.wait_ge(sm.s_sc, gi + 1)
                    if qb >= 1 and kc == 0:
                        if PH3_LVL >= 2:
                            sc.wait_ge(sm.s_cx, 8 * qb)
                        if PH3_LVL >= 1:
                            sc.wait_ge(sm.s_spp, 32 * qb)
                    slot = kc % 4
                    pbs = PB[slot % 2][:, (slot // 2) * 256:(slot // 2 + 1) * 256]
                    sc.activation(probs[:, kc * QBS:(kc + 1) * QBS], pbs, AF.Exp,
                                  scale=1.0 / 32.0).then_inc(sm.s_exp, 1)

        @Bl.vector
        def _(v):
            for qb in range(NQB):
                for kc in range(KC):
                    gi = qb * KC + kc
                    v.wait_ge(sm.s_exp, gi + 1)
                    if kc == 0:
                        ins = v.tensor_copy(spp[:], probs[:, 0:QBS])
                    else:
                        ins = v.tensor_add(spp[:], spp[:], probs[:, kc * QBS:(kc + 1) * QBS])
                    ins.then_inc(sm.s_spp, 1)
                if PH3_LVL >= 3:
                    v.wait_ge(sm.s_ts3, qb + 1)
                    v.reciprocal(s_inv[:, 0:2], PB[6][:, 0:2])
                    v.drain().then_inc(sm.s_siv, 1)
                if PH3_LVL >= 2:
                    v.wait_ge(sm.s_cx, 8 * (qb + 1))
                    for b in range(4):
                        ins = v.tensor_copy(ctx_sb[:, b * 512:(b + 1) * 512], PB[2 + b][:])
                    ins.then_inc(sm.s_cve, 1)
                if PH3_LVL < 4 or PH3_NOEVAC:
                    continue
                if qb >= 1:
                    v.wait_ge(sm.s_ao, 16 * qb)
                for qq in range(2):
                    v.wait_ge(sm.s_to, qb * 2 + qq + 1)
                    for h in range(2):
                        if os.environ.get("PH3_EVAC", "stt") == "copy":
                            ins = v.tensor_copy(
                                attn_st[:, qq * 1024 + h * 512: qq * 1024 + (h + 1) * 512],
                                PB[2 + qq * 2 + h][:])
                        else:
                            ins = v.scalar_tensor_tensor(
                                attn_st[:, qq * 1024 + h * 512: qq * 1024 + (h + 1) * 512],
                                PB[2 + qq * 2 + h][:], s_inv[:, qq:qq + 1],
                                bo_bc[:, h * 512:(h + 1) * 512], ALU.mult, ALU.add)
                    ins.then_inc(sm.s_ave, 2)

    if phases < 4:
        emit_dbg()
        return nc

    def m1_bank_wait(t, idx, gidx, tbg):
        if idx == 0 and gidx == 0:
            return
        if gidx == 0:
            t.wait_ge(sm.s_ge, (idx - 1) * 12 + 10 + tbg + 1)
        elif gidx % 2 == 0:
            t.wait_ge(sm.s_ge, idx * 12 + gidx - 2 + tbg + 1)
        else:
            t.wait_ge(sm.s_ye, idx * 12 + gidx - 1 + tbg + 1)

    with nc.Block("ph4") as Bl:

        @Bl.gpsimd
        def _(g):
            w1v = w1a_d.rearrange("e (dc p) j -> e p dc j", p=128)
            w2v = w2a_d.rearrange("e (hc p) j -> e p hc j", p=128)
            g.dma_start(w2_buf[:].rearrange("p (hc j) -> p hc j", hc=HC2),
                        w2v[0]).then_inc(sm.s_w2, 16)
            for idx in range(1, 2 * NE):
                e = idx % NE
                g.wait_ge(sm.s_m1f, idx)
                g.dma_start(w1_buf[:].rearrange("p (dc j) -> p dc j", dc=DC),
                            w1v[e]).then_inc(sm.s_w1, 16)
                g.wait_ge(sm.s_m2f, idx)
                g.dma_start(w2_buf[:].rearrange("p (hc j) -> p hc j", hc=HC2),
                            w2v[e]).then_inc(sm.s_w2, 16)

        @Bl.sync
        def _(sy):
            xs = x_d.rearrange("(tt p) d -> tt p d", p=128)
            ai = ah.rearrange("(tt p) d -> tt p d", p=128)
            oo = out_d.rearrange("(tt p) d -> tt p d", p=128)
            xf = xfh.rearrange("dc p (hf t) -> hf p dc t", hf=2)
            sy.wait_ge(sm.s_ao, 16 * NQB)

            def prefetch(gt):
                if gt >= 2:
                    sy.wait_ge(sm.s_cmb, gt - 1)
                sy.dma_start(x_re[gt % 2][:], xs[gt]).then_inc(sm.s_x2, 16)
                sy.dma_start(attn_re[gt % 2][:], ai[gt]).then_inc(sm.s_ai, 16)

            for half in range(2):
                if half == 1:
                    sy.wait_ge(sm.s_m1f, NE)
                sy.dma_start(xh_sb[:].rearrange("p (dc t) -> p dc t", dc=DC),
                             xf[half]).then_inc(sm.s_xh, 16)
                prefetch(half * TTH)
                prefetch(half * TTH + 1)
                for tt in range(TTH):
                    gt = half * TTH + tt
                    sy.wait_ge(sm.s_cmb, gt + 1)
                    sy.dma_start(oo[gt], out_st[:]).then_inc(sm.s_out, 16)
                    if tt + 2 < TTH:
                        prefetch(gt + 2)
            sy.wait_ge(sm.s_out, 16 * 32)

        @Bl.tensor
        def _(t):
            for half in range(2):
                for e in range(NE):
                    idx = half * NE + e
                    t.wait_ge(sm.s_w1, 16 * (idx + 1))
                    t.wait_ge(sm.s_xh, 16 * (half + 1))
                    for gidx, hc in enumerate(HCORD):
                        for tbg in range(2):
                            gg = gidx * 2 + tbg
                            m1_bank_wait(t, idx, gidx, tbg)
                            for dc in range(DC):
                                lhs = w1_buf[:, dc * 1536 + hc * 128: dc * 1536 + (hc + 1) * 128]
                                t.ldweights(lhs)
                                for tb in range(2):
                                    ins = t.matmul(PB[tbg * 2 + tb][:], lhs,
                                                   xh_sb[:, dc * 2048 + tbg * 1024 + tb * 512:
                                                         dc * 2048 + tbg * 1024 + (tb + 1) * 512],
                                                   start=(dc == 0), stop=(dc == DC - 1))
                            if gg == 23:
                                ins.then_inc(sm.s_m1g, 1)
                                t.nop().then_inc(sm.s_m1f, 1)
                            else:
                                ins.then_inc(sm.s_m1g, 1)
                    t.wait_ge(sm.s_w2, 16 * (idx + 1))
                    t.wait_ge(sm.s_act, idx * HC2 + HC2)
                    for tt in range(TTH):
                        zi = idx * TTH + tt
                        if zi >= 2:
                            t.wait_ge(sm.s_zadd, zi - 1)
                        for hc in range(HC2):
                            lhs = y_sb[:, hc * 2048 + tt * 128: hc * 2048 + (tt + 1) * 128]
                            t.ldweights(lhs)
                            for h in range(2):
                                ins = t.matmul(PB[4 + (tt % 2) * 2 + h][:], lhs,
                                               w2_buf[:, hc * 1024 + h * 512: hc * 1024 + (h + 1) * 512],
                                               start=(hc == 0), stop=(hc == HC2 - 1))
                        if tt == TTH - 1:
                            ins.then_inc(sm.s_m2t, 1)
                            t.nop().then_inc(sm.s_m2f, 1)
                        else:
                            ins.then_inc(sm.s_m2t, 1)

        @Bl.scalar
        def _(sc):
            for half in range(2):
                for e in range(NE):
                    idx = half * NE + e
                    for gidx, hc in enumerate(HCORD):
                        if hc < HC2:
                            continue
                        for tbg in range(2):
                            gg = gidx * 2 + tbg
                            sc.wait_ge(sm.s_m1g, idx * 24 + gg + 1)
                            for tb in range(2):
                                ins = sc.activation(
                                    sg_sb[:, (hc - HC2) * 2048 + tbg * 1024 + tb * 512:
                                          (hc - HC2) * 2048 + tbg * 1024 + (tb + 1) * 512],
                                    PB[tbg * 2 + tb][:], AF.Silu)
                            ins.then_inc(sm.s_ge, 1)

        @Bl.vector
        def _(v):
            for half in range(2):
                for e in range(NE):
                    idx = half * NE + e
                    for gidx, hc in enumerate(HCORD):
                        if hc >= HC2:
                            continue
                        for tbg in range(2):
                            gg = gidx * 2 + tbg
                            v.wait_ge(sm.s_m1g, idx * 24 + gg + 1)
                            for tb in range(2):
                                ins = v.tensor_copy(
                                    y_sb[:, hc * 2048 + tbg * 1024 + tb * 512:
                                         hc * 2048 + tbg * 1024 + (tb + 1) * 512],
                                    PB[tbg * 2 + tb][:])
                            ins.then_inc(sm.s_ye, 1)
                        v.wait_ge(sm.s_ge, idx * 12 + (hc + 1) * 2)
                        v.tensor_mul(y_sb[:, hc * 2048:(hc + 1) * 2048],
                                     y_sb[:, hc * 2048:(hc + 1) * 2048],
                                     sg_sb[:, hc * 2048:(hc + 1) * 2048]).then_inc(sm.s_act, 1)
                    for tt in range(TTH):
                        zi = idx * TTH + tt
                        gt = half * TTH + tt
                        v.wait_ge(sm.s_m2t, zi + 1)
                        wcol = w_all[:, gt * 8 + min(e, 7): gt * 8 + min(e, 7) + 1]
                        for h in range(2):
                            dst = z_sb[:, tt * 1024 + h * 512: tt * 1024 + (h + 1) * 512]
                            pbm = PB[4 + (tt % 2) * 2 + h][:]
                            if e == 0:
                                ins = v.tensor_scalar(dst, pbm, wcol, None, ALU.mult)
                            elif e < 8:
                                ins = v.scalar_tensor_tensor(dst, pbm, wcol, dst, ALU.mult, ALU.add)
                            else:
                                ins = v.tensor_tensor(dst, pbm, dst, ALU.add)
                        ins.then_inc(sm.s_zadd, 1)
                for tt in range(TTH):
                    gt = half * TTH + tt
                    v.wait_ge(sm.s_x2, 16 * (gt + 1))
                    v.wait_ge(sm.s_ai, 16 * (gt + 1))
                    if gt > 0:
                        v.wait_ge(sm.s_out, 16 * gt)
                    v.tensor_add(out_st[:], x_re[gt % 2][:], z_sb[:, tt * 1024:(tt + 1) * 1024])
                    v.tensor_add(out_st[:], out_st[:], attn_re[gt % 2][:]).then_inc(sm.s_cmb, 1)

    emit_dbg()
    return nc


_NC_CACHE = {}


def _get_nc(phases=4, debug=False):
    key = (phases, debug)
    if key not in _NC_CACHE:
        _NC_CACHE[key] = build(phases, debug)
    return _NC_CACHE[key]


def prep_inputs(L0_attn_x, L0_attn_W_qkv, L0_attn_b_qkv, L0_attn_W_out, L0_attn_b_out,
                L0_ffn_gate_w, L0_ffn_W1, L0_ffn_W2, L0_ffn_Ws1, L0_ffn_Ws2):
    f = np.float32
    x = np.ascontiguousarray(np.asarray(L0_attn_x, dtype=f))
    wqkv = np.ascontiguousarray(np.asarray(L0_attn_W_qkv, dtype=f))
    bqkv = np.asarray(L0_attn_b_qkv, dtype=f)
    bqk = np.ascontiguousarray(bqkv[:2048].reshape(16, 128).T)
    bv = np.ascontiguousarray(bqkv[2048:].reshape(1, 1024))
    wout = np.ascontiguousarray(np.asarray(L0_attn_W_out, dtype=f))
    bo = np.ascontiguousarray(np.asarray(L0_attn_b_out, dtype=f).reshape(1, 1024))
    gwt = np.ascontiguousarray(np.asarray(L0_ffn_gate_w, dtype=f).T)
    W1 = np.asarray(L0_ffn_W1, dtype=f)
    W2 = np.asarray(L0_ffn_W2, dtype=f)
    Ws1 = np.asarray(L0_ffn_Ws1, dtype=f)
    Ws2 = np.asarray(L0_ffn_Ws2, dtype=f)

    w1a = np.empty((NE, D, 2 * HEXP), dtype=f)
    w2a = np.empty((NE, HEXP, D), dtype=f)
    for e in range(8):
        w1a[e] = W1[e].T
        w2a[e] = W2[e].T
    ws1t = Ws1.T
    ws2t = Ws2.T
    w1a[8] = np.concatenate([ws1t[:, 0:768], ws1t[:, 1536:2304]], axis=1)
    w1a[9] = np.concatenate([ws1t[:, 768:1536], ws1t[:, 2304:3072]], axis=1)
    w2a[8] = ws2t[0:768]
    w2a[9] = ws2t[768:1536]
    w1a = np.ascontiguousarray(w1a)
    w2a = np.ascontiguousarray(w2a)

    shared = {"wqkv": wqkv, "bqk": bqk, "bv": bv, "bo": bo, "wout": wout,
              "gwt": gwt, "w1a": w1a, "w2a": w2a}
    in_maps = []
    for c in range(8):
        m = {"x": np.ascontiguousarray(x[c])}
        m.update(shared)
        in_maps.append(m)
    return in_maps


def kernel(**inputs):
    in_maps = prep_inputs(**inputs)
    nc = _get_nc()
    res = run_bass_kernel_spmd(nc, in_maps, core_ids=list(range(8)))
    return np.stack([r["out"] for r in res.results], axis=0)


# version 02
# speedup vs baseline: 1.0541x; 1.0541x over previous
import os
import numpy as np
import ml_dtypes

import concourse.bass as bass
import concourse.mybir as mybir
from concourse.bass_utils import run_bass_kernel_spmd

F32 = mybir.dt.float32
BF16 = mybir.dt.bfloat16
AF = mybir.ActivationFunctionType
ALU = mybir.AluOpType
AX = mybir.AxisListType
bfdt = ml_dtypes.bfloat16

S = 4096
D = 1024
DC = 8
NTILE = 32
NSLAB = 8
QBS = 256
NQB = S // QBS
KC = 32
NE = 10
HEXP = 768
HC1 = 12
HC2 = 6
NHALF = 2048
TTH = 16

BASE = 16512
HCORD = [0, 6, 1, 7, 2, 8, 3, 9, 4, 10, 5, 11]


def build(phases=4, debug=False):
    nc = bass.Bass("TRN2", target_bir_lowering=False)

    x_d = nc.dram_tensor("x", [S, D], F32, kind="ExternalInput").ap()
    wqkv_d = nc.dram_tensor("wqkv", [D, 3 * D], F32, kind="ExternalInput").ap()
    bqk_d = nc.dram_tensor("bqk", [128, 16], F32, kind="ExternalInput").ap()
    bv_d = nc.dram_tensor("bv", [1, D], F32, kind="ExternalInput").ap()
    bo_d = nc.dram_tensor("bo", [1, D], F32, kind="ExternalInput").ap()
    wout_d = nc.dram_tensor("wout", [D, D], F32, kind="ExternalInput").ap()
    gwt_d = nc.dram_tensor("gwt", [D, 8], F32, kind="ExternalInput").ap()
    w1a_d = nc.dram_tensor("w1a", [NE, D, 2 * HEXP], F32, kind="ExternalInput").ap()
    w2a_d = nc.dram_tensor("w2a", [NE, HEXP, D], F32, kind="ExternalInput").ap()
    out_d = nc.dram_tensor("out", [S, D], F32, kind="ExternalOutput").ap()

    dbg_kind = "ExternalOutput" if debug else "Internal"
    qh = nc.dram_tensor("q_hbm", [DC, 128, S], BF16, kind=dbg_kind).ap()
    vh = nc.dram_tensor("v_hbm", [S, D], BF16, kind=dbg_kind).ap()
    ah = nc.dram_tensor("attn_hbm", [S, D], BF16, kind=dbg_kind).ap()
    xfh = nc.dram_tensor("xnf_hbm", [DC, 128, S], BF16, kind=dbg_kind).ap()
    if debug:
        dbgw_d = nc.dram_tensor("dbg_w", [128, NTILE * 8], F32, kind="ExternalOutput").ap()

    idb = nc.inline_tensor(np.eye(128, dtype=bfdt), name="ident_bf").ap()
    i8f = nc.inline_tensor(np.eye(8, dtype=np.float32), name="ident8_f").ap()
    onesc = nc.inline_tensor(np.ones((128, 1), dtype=np.float32), name="ones_col").ap()
    onesr = nc.inline_tensor(np.ones((1, 128), dtype=np.float32), name="ones_row").ap()

    def at(name, shape, dtype, off):
        assert off >= BASE, name
        nbytes = int(np.prod(shape[1:])) * (2 if dtype == BF16 else 4)
        assert off + nbytes <= 229344, (name, off, nbytes)
        return nc.alloc_sbuf_tensor_at(name, shape, dtype, offset=off)

    A = BASE
    B = BASE + 65536
    C = BASE + 131072
    DD = BASE + 180224

    xnf = at("xnf", [128, DC * S], BF16, A)
    v_sb = at("v_sb", [128, NTILE * D], BF16, A)
    z_sb = at("z_sb", [128, TTH * D], F32, A)

    x_buf = [at(f"x_buf{i}", [128, 4 * D], F32, B + i * 16384) for i in range(2)]
    xn32 = [at(f"xn32_{i}", [128, D], F32, B + 32768 + i * 4096) for i in range(2)]
    hi_tok = [at(f"hi{i}", [128, D], BF16, B + 40960 + i * 2048) for i in range(2)]
    lo_tok = [at(f"lo{i}", [128, D], BF16, B + 45056 + i * 2048) for i in range(2)]
    lo_f = at("lof", [128, DC * 512], BF16, B + 49152)
    k_sb = at("k_sb", [128, DC * S], BF16, B)
    y_sb = at("y_sb", [128, HC2 * NHALF], BF16, B)
    sg_sb = at("sg_sb", [128, HC2 * NHALF], BF16, B + 24576)
    w2_buf = at("w2_buf", [128, HC2 * D], BF16, B + 49152)
    brow = at("brow", [1, D], F32, B + 57344)
    lgT = at("lgT", [8, 512], F32, B + 61440)
    junk = at("junk", [128, D], BF16, B + 63488)
    probs_all = at("probs_all", [128, NTILE * 8], F32, B + 63488)
    msk_all = at("msk_all", [128, NTILE * 8], F32, B + 57344)
    l2_all = at("l2_all", [128, NTILE * 8], F32, B + 58368)

    wqkv_bf = at("wqkv_bf", [128, DC * 3 * D], BF16, C)
    wout_bf = at("wout_bf", [128, DC * D], BF16, C)
    qf_sb = at("qf_sb", [128, DC * QBS], BF16, C + 16384)
    probs = at("probs", [128, KC * QBS], BF16, C + 20480)
    spp = at("spp", [128, QBS], F32, C + 36864)
    ctx_sb = at("ctx_sb", [128, DC * QBS], BF16, C + 37888)
    attn_st = at("attn_st", [128, 2 * D], BF16, C + 41984)
    xh_sb = at("xh_sb", [128, DC * NHALF], BF16, C)
    x_re = [at(f"x_re{i}", [128, D], F32, C + 32768 + i * 4096) for i in range(2)]
    attn_re = [at(f"attn_re{i}", [128, D], BF16, C + 40960 + i * 2048) for i in range(2)]
    out_st = at("out_st", [128, D], F32, C + 45056)

    w1_buf = at("w1_buf", [128, DC * 2 * HEXP], BF16, DD)
    qstage = [at(f"qstage{i}", [128, 2048], BF16, DD + i * 4096) for i in range(2)]
    vstage = [at(f"vstage{i}", [128, D], BF16, DD + 8192 + i * 2048) for i in range(2)]
    o = DD + 24576
    w_all = at("w_all", [128, NTILE * 8], F32, o); o += 1024
    gwt_f = at("gwt_f", [128, DC * 8], F32, o); o += 256
    gw_hi = at("gw_hi", [128, DC * 8], BF16, o); o += 128
    gw_lo = at("gw_lo", [128, DC * 8], BF16, o); o += 128
    b_sb = at("b_sb", [128, 16], F32, o); o += 64
    id_sb = at("id_sb", [128, 128], BF16, o); o += 256
    i8_sb = at("i8_sb", [8, 8], F32, o); o += 32
    onec_sb = at("onec_sb", [128, 1], F32, o); o += 32
    oner_sb = at("oner_sb", [1, 128], F32, o); o += 512
    lg_tok = at("lg_tok", [128, NTILE * 8], F32, o); o += 1024
    stt = at("stt", [128, 64], F32, o); o += 256
    bv_bc = at("bv_bc", [128, D], BF16, o); o += 2048
    bo_bc = at("bo_bc", [128, D], BF16, o); o += 2048
    s_inv = at("s_inv", [128, 4], F32, o); o += 32
    assert o <= 229344, o

    PB = [nc.alloc_psum_tensor(f"pb{i}", [128, 512], F32) for i in range(8)]

    names = [
        "s_cst", "s_x1", "s_wq", "s_xnfo", "s_qo", "s_vo", "s_vi", "s_qf", "s_ao",
        "s_wo", "s_w1", "s_w2", "s_xh", "s_x2", "s_ai", "s_out", "s_bvbo",
        "s_v1", "s_vu", "s_c1", "s_vhi", "s_th", "s_tl", "s_v1e", "s_vloe",
        "s_tg", "s_vlg", "s_tk", "s_vtok", "s_tb", "s_vgw", "s_gsm", "s_c1a",
        "s_ev2", "s_t2",
        "s_sc", "s_exp", "s_spp", "s_cx", "s_cve", "s_ts3", "s_siv", "s_to", "s_ave",
        "s_m1g", "s_ye", "s_ge", "s_m1f", "s_act", "s_m2t", "s_m2f", "s_zadd", "s_cmb",
    ]
    sm = type("SM", (), {n: nc.alloc_semaphore(n) for n in names})


    def emit_dbg():
        if not debug:
            return
        with nc.Block("dbg") as Bl:
            @Bl.sync
            def _(sy):
                sy.dma_start(dbgw_d, w_all[:]).then_inc(sm.s_cst, 16)
                sy.wait_ge(sm.s_cst, 16 * 7)

    def xnf3(apx):
        return apx.rearrange("p (dc t) -> p dc t", dc=DC)

    with nc.Block("ph1") as Bl:

        @Bl.sync
        def _(sy):
            sy.dma_start(id_sb[:], idb).then_inc(sm.s_cst, 16)
            sy.dma_start(i8_sb[:], i8f).then_inc(sm.s_cst, 16)
            sy.dma_start(onec_sb[:], onesc).then_inc(sm.s_cst, 16)
            sy.dma_start(oner_sb[:], onesr).then_inc(sm.s_cst, 16)
            sy.dma_start(gwt_f[:].rearrange("p (dc e) -> p dc e", dc=DC),
                         gwt_d.rearrange("(dc p) e -> p dc e", p=128)).then_inc(sm.s_cst, 16)
            sy.dma_start(b_sb[:], bqk_d).then_inc(sm.s_cst, 16)
            xs = x_d.rearrange("(s q p) d -> s p q d", p=128, q=4)
            for s in range(NSLAB):
                if s >= 2:
                    sy.wait_ge(sm.s_v1, 4 * s - 4)
                sy.dma_start(x_buf[s % 2][:].rearrange("p (q d) -> p q d", q=4),
                             xs[s]).then_inc(sm.s_x1, 16)
            xo = xfh.rearrange("dc p (sl t) -> sl p dc t", sl=NSLAB)
            for s in range(NSLAB):
                sy.wait_ge(sm.s_v1e, 4 * s + 4)
                sy.dma_start(xo[s],
                             xnf3(xnf[:])[:, :, s * 512:(s + 1) * 512]).then_inc(sm.s_xnfo, 16)
            sy.dma_start(brow[:], bv_d).then_inc(sm.s_bvbo, 16)
            sy.wait_ge(sm.s_tb, 1)
            sy.dma_start(brow[:], bo_d).then_inc(sm.s_bvbo, 16)

        @Bl.gpsimd
        def _(g):
            g.dma_start(wqkv_bf[:].rearrange("p (dc j) -> p dc j", dc=DC),
                        wqkv_d.rearrange("(dc p) j -> p dc j", p=128)).then_inc(sm.s_wq, 16)

        @Bl.vector
        def _(v):
            v.wait_ge(sm.s_cst, 16 * 6)
            v.tensor_copy(gw_hi[:], gwt_f[:])
            v.tensor_sub(gw_lo[:], gwt_f[:], gw_hi[:])
            v.drain().then_inc(sm.s_vgw, 1)
            for sl in range(NSLAB):
                c0 = (sl % 2) * 32
                v.wait_ge(sm.s_x1, 16 * (sl + 1))
                for q in range(4):
                    xcol = x_buf[sl % 2][:, q * D:(q + 1) * D]
                    v.reduce_sum(stt[:, c0 + q:c0 + q + 1], xcol, AX.X)
                    v.scalar_tensor_tensor(junk[:], xcol, 1.0, xcol, ALU.mult, ALU.mult,
                                           accum_out=stt[:, c0 + 4 + q:c0 + 5 + q])
                v.drain()
                v.tensor_scalar(stt[:, c0 + 8:c0 + 12], stt[:, c0:c0 + 4], 1.0 / D, None, ALU.mult)
                v.tensor_scalar(stt[:, c0 + 12:c0 + 16], stt[:, c0 + 4:c0 + 8], 1.0 / D, 1e-5,
                                ALU.mult, ALU.add)
                v.drain()
                v.tensor_mul(stt[:, c0 + 16:c0 + 20], stt[:, c0 + 8:c0 + 12], stt[:, c0 + 8:c0 + 12])
                v.drain()
                v.scalar_tensor_tensor(stt[:, c0 + 12:c0 + 16], stt[:, c0 + 16:c0 + 20], -1.0,
                                       stt[:, c0 + 12:c0 + 16], ALU.mult, ALU.add)
                v.drain()
                v.reciprocal(stt[:, c0 + 16:c0 + 20], stt[:, c0 + 12:c0 + 16])
                v.drain().then_inc(sm.s_vu, 1)
                v.wait_ge(sm.s_c1a, sl + 1)
                v.tensor_mul(stt[:, c0 + 24:c0 + 28], stt[:, c0 + 20:c0 + 24], stt[:, c0 + 20:c0 + 24])
                v.drain()
                v.tensor_mul(stt[:, c0 + 24:c0 + 28], stt[:, c0 + 24:c0 + 28], stt[:, c0 + 12:c0 + 16])
                v.drain()
                v.tensor_scalar(stt[:, c0 + 24:c0 + 28], stt[:, c0 + 24:c0 + 28], -0.5, 1.5,
                                ALU.mult, ALU.add)
                v.drain()
                v.tensor_mul(stt[:, c0 + 24:c0 + 28], stt[:, c0 + 24:c0 + 28], stt[:, c0 + 20:c0 + 24])
                v.drain()
                for q in range(4):
                    ti = sl * 4 + q
                    p = ti % 2
                    xcol = x_buf[sl % 2][:, q * D:(q + 1) * D]
                    if ti >= 2:
                        v.wait_ge(sm.s_c1, ti - 1)
                    v.tensor_scalar(xn32[p][:], xcol, stt[:, c0 + 8 + q:c0 + 9 + q],
                                    stt[:, c0 + 24 + q:c0 + 25 + q],
                                    ALU.subtract, ALU.mult)
                    v.drain().then_inc(sm.s_v1, 1)
                    v.wait_ge(sm.s_c1, ti + 1)
                    if ti >= 2:
                        v.wait_ge(sm.s_tl, ti - 1)
                    v.tensor_sub(lo_tok[p][:], xn32[p][:], hi_tok[p][:])
                    v.drain().then_inc(sm.s_vhi, 1)
                v.wait_ge(sm.s_tg, sl + 1)
                v.tensor_copy(lgT[:], PB[0][0:8, :])
                v.drain().then_inc(sm.s_vlg, 1)
                v.wait_ge(sm.s_tk, sl + 1)
                v.tensor_copy(lg_tok[:, sl * 32:(sl + 1) * 32], PB[1][:, 0:32])
                v.drain().then_inc(sm.s_vtok, 1)
            v.wait_ge(sm.s_tb, 1)
            v.tensor_copy(bv_bc[:, 0:512], PB[0][:])
            v.tensor_copy(bv_bc[:, 512:1024], PB[1][:])
            v.wait_ge(sm.s_tb, 4)
            v.tensor_copy(bo_bc[:, 0:512], PB[2][:])
            v.tensor_copy(bo_bc[:, 512:1024], PB[3][:])
            v.wait_ge(sm.s_gsm, 1)
            lg3 = lg_tok[:].rearrange("p (g k) -> p g k", g=NTILE)
            pr3 = probs_all[:].rearrange("p (g k) -> p g k", g=NTILE)
            mk3 = msk_all[:].rearrange("p (g k) -> p g k", g=NTILE)
            l23 = l2_all[:].rearrange("p (g k) -> p g k", g=NTILE)
            wa3 = w_all[:].rearrange("p (g k) -> p g k", g=NTILE)

            def bc(col_ap):
                a3 = col_ap.rearrange("p (g o) -> p g o", o=1)
                import concourse.ap as _cap
                return _cap.AP(a3.tensor, a3.offset, [a3.ap[0], a3.ap[1], [0, 8]])

            v.reduce_sum(stt[:, 0:NTILE], pr3, AX.X)
            v.drain()
            v.reciprocal(stt[:, 32:32 + NTILE], stt[:, 0:NTILE])
            v.reduce_max(stt[:, 0:NTILE], lg3, AX.X)
            v.drain()
            v.tensor_tensor(mk3, lg3, bc(stt[:, 0:NTILE]), ALU.is_ge)
            v.drain()
            v.scalar_tensor_tensor(l2_all[:], msk_all[:], -1e30, lg_tok[:], ALU.mult, ALU.add)
            v.drain()
            v.reduce_max(stt[:, 0:NTILE], l23, AX.X)
            v.drain()
            v.tensor_tensor(mk3, lg3, bc(stt[:, 0:NTILE]), ALU.is_ge)
            v.drain()
            v.tensor_mul(msk_all[:], msk_all[:], probs_all[:])
            v.drain()
            v.tensor_tensor(wa3, mk3, bc(stt[:, 32:32 + NTILE]), ALU.mult)
            v.drain().then_inc(sm.s_v1, 1)

        @Bl.scalar
        def _(sc):
            for ti in range(NTILE):
                sl, q = divmod(ti, 4)
                p = ti % 2
                c0 = (sl % 2) * 32
                if q == 0:
                    sc.wait_ge(sm.s_vu, sl + 1)
                    sc.sqrt(stt[:, c0 + 20:c0 + 24], stt[:, c0 + 16:c0 + 20])
                    sc.drain().then_inc(sm.s_c1a, 1)
                sc.wait_ge(sm.s_v1, ti + 1)
                sc.copy(hi_tok[p][:], xn32[p][:])
                sc.drain().then_inc(sm.s_c1, 1)
                sc.wait_ge(sm.s_th, ti + 1)
                dsth = xnf3(xnf[:])[:, :, ti * 128:(ti + 1) * 128]
                sc.copy(dsth[:, 0:4, :], PB[p * 2][:].rearrange("p (a t) -> p a t", a=4))
                sc.copy(dsth[:, 4:8, :],
                        PB[p * 2 + 1][:].rearrange("p (a t) -> p a t", a=4)).then_inc(sm.s_v1e, 1)
                sc.wait_ge(sm.s_tl, ti + 1)
                if ti >= 4:
                    sc.wait_ge(sm.s_tg, ti // 4)
                dst = lo_f[:].rearrange("p (dc t) -> p dc t", dc=DC)[:, :, q * 128:(q + 1) * 128]
                sc.copy(dst[:, 0:4, :], PB[4 + p * 2][:].rearrange("p (a t) -> p a t", a=4))
                sc.copy(dst[:, 4:8, :],
                        PB[4 + p * 2 + 1][:].rearrange("p (a t) -> p a t", a=4)).then_inc(sm.s_vloe, 1)
            sc.wait_ge(sm.s_vtok, NSLAB)
            sc.wait_ge(sm.s_tg, NSLAB)
            sc.wait_ge(sm.s_tb, 4)
            sc.activation(probs_all[:], lg_tok[:], AF.Exp)
            sc.drain().then_inc(sm.s_gsm, 1)

        @Bl.tensor
        def _(t):
            t.wait_ge(sm.s_cst, 16 * 6)
            for ti in range(NTILE):
                p = ti % 2
                s, q = divmod(ti, 4)
                t.wait_ge(sm.s_c1, ti + 1)
                if ti >= 2:
                    t.wait_ge(sm.s_v1e, ti - 1)
                if q == 0 and s >= 1 and p == 0:
                    t.wait_ge(sm.s_vlg, s)
                    t.wait_ge(sm.s_vtok, s)
                for dc in range(DC):
                    lhs = hi_tok[p][:, dc * 128:(dc + 1) * 128]
                    t.ldweights(lhs)
                    ins = t.matmul(PB[p * 2 + dc // 4][:, (dc % 4) * 128:(dc % 4 + 1) * 128],
                                   lhs, id_sb[:], start=True, stop=True)
                ins.then_inc(sm.s_th, 1)
                t.wait_ge(sm.s_vhi, ti + 1)
                if ti >= 2:
                    t.wait_ge(sm.s_vloe, ti - 1)
                for dc in range(DC):
                    lhs = lo_tok[p][:, dc * 128:(dc + 1) * 128]
                    t.ldweights(lhs)
                    ins = t.matmul(PB[4 + p * 2 + dc // 4][:, (dc % 4) * 128:(dc % 4 + 1) * 128],
                                   lhs, id_sb[:], start=True, stop=True)
                ins.then_inc(sm.s_tl, 1)
                if q == 3:
                    t.wait_ge(sm.s_vgw, 1)
                    t.wait_ge(sm.s_v1e, 4 * s + 4)
                    t.wait_ge(sm.s_vloe, 4 * s + 4)
                    if s >= 1:
                        t.wait_ge(sm.s_vlg, s)
                    nmm = 0
                    for (gw, mvsrc) in ((gw_hi, "hi"), (gw_lo, "hi"), (gw_hi, "lo")):
                        for dc in range(DC):
                            t.ldweights(gw[:, dc * 8:(dc + 1) * 8])
                            if mvsrc == "hi":
                                mv = xnf3(xnf[:])[:, dc, s * 512:(s + 1) * 512]
                            else:
                                mv = lo_f[:, dc * 512:(dc + 1) * 512]
                            ins = t.matmul(PB[0][0:8, :], gw[:, dc * 8:(dc + 1) * 8], mv,
                                           start=(nmm == 0), stop=(nmm == 23))
                            nmm += 1
                    ins.then_inc(sm.s_tg, 1)
                    t.wait_ge(sm.s_vlg, s + 1)
                    if s >= 1:
                        t.wait_ge(sm.s_vtok, s)
                    for qq in range(4):
                        ins = t.matmul(PB[1][:, qq * 8:(qq + 1) * 8],
                                       lgT[0:8, qq * 128:(qq + 1) * 128], i8_sb[:],
                                       start=True, stop=True)
                    ins.then_inc(sm.s_tk, 1)
            t.wait_ge(sm.s_bvbo, 16)
            t.wait_ge(sm.s_v1e, NTILE)
            t.wait_ge(sm.s_vlg, NSLAB)
            t.wait_ge(sm.s_vtok, NSLAB)
            t.matmul(PB[0][:], oner_sb[:], brow[:, 0:512], start=True, stop=True)
            t.matmul(PB[1][:], oner_sb[:], brow[:, 512:1024], start=True, stop=True).then_inc(sm.s_tb, 1)
            t.wait_ge(sm.s_bvbo, 32)
            t.wait_ge(sm.s_vloe, NTILE)
            t.matmul(PB[2][:], oner_sb[:], brow[:, 0:512], start=True, stop=True)
            t.matmul(PB[3][:], oner_sb[:], brow[:, 512:1024], start=True, stop=True).then_inc(sm.s_tb, 3)

    if phases < 2:
        emit_dbg()
        return nc

    with nc.Block("ph2") as Bl:

        @Bl.tensor
        def _(t):
            t.wait_ge(sm.s_wq, 16)
            for u in range(32):
                oc, tbg = divmod(u, 2)
                quad = u % 2
                if u >= 2:
                    t.wait_ge(sm.s_ev2, 4 * (u - 1))
                for dc in range(DC):
                    lhs = wqkv_bf[:, dc * 3072 + oc * 128: dc * 3072 + (oc + 1) * 128]
                    t.ldweights(lhs)
                    for tb in range(4):
                        ins = t.matmul(PB[quad * 4 + tb][:], lhs,
                                       xnf3(xnf[:])[:, dc, (tbg * 4 + tb) * 512:(tbg * 4 + tb + 1) * 512],
                                       start=(dc == 0), stop=(dc == DC - 1))
                ins.then_inc(sm.s_t2, 1)
            for tt in range(NTILE):
                pr = tt % 2
                if tt < 2:
                    t.wait_ge(sm.s_ev2, 124)
                else:
                    t.wait_ge(sm.s_ev2, 128 + 2 * (tt - 1))
                for dc in range(DC):
                    lhs = xnf3(xnf[:])[:, dc, tt * 128:(tt + 1) * 128]
                    t.ldweights(lhs)
                    for h in range(2):
                        ins = t.matmul(PB[pr * 2 + h][:], lhs,
                                       wqkv_bf[:, dc * 3072 + 2048 + h * 512: dc * 3072 + 2048 + (h + 1) * 512],
                                       start=(dc == 0), stop=(dc == DC - 1))
                ins.then_inc(sm.s_t2, 1)

        @Bl.scalar
        def _(sc):
            for u in range(32):
                oc, tbg = divmod(u, 2)
                quad = u % 2
                sc.wait_ge(sm.s_t2, u + 1)
                if oc < 8 and u >= 2:
                    sc.wait_ge(sm.s_qo, 16 * (u - 1))
                for tb in range(4):
                    tok = (tbg * 4 + tb) * 512
                    if oc < 8:
                        dst = qstage[u % 2][:, tb * 512:(tb + 1) * 512]
                    else:
                        dst = k_sb[:, (oc - 8) * 4096 + tok:(oc - 8) * 4096 + tok + 512]
                    ins = sc.activation(dst, PB[quad * 4 + tb][:], AF.Identity,
                                        bias=b_sb[:, oc:oc + 1])
                ins.then_inc(sm.s_ev2, 4)

        @Bl.vector
        def _(v):
            for tt in range(NTILE):
                pr = tt % 2
                v.wait_ge(sm.s_t2, 32 + tt + 1)
                if tt >= 2:
                    v.wait_ge(sm.s_vo, 16 * (tt - 1))
                for h in range(2):
                    ins = v.tensor_tensor(vstage[pr][:, h * 512:(h + 1) * 512],
                                          PB[pr * 2 + h][:], bv_bc[:, h * 512:(h + 1) * 512], ALU.add)
                ins.then_inc(sm.s_ev2, 2)

        @Bl.sync
        def _(sy):
            qo = qh.rearrange("dc p (g t) -> dc g p t", g=2)
            for u in range(16):
                sy.wait_ge(sm.s_ev2, 4 * (u + 1))
                sy.dma_start(qo[u // 2, u % 2], qstage[u % 2][:]).then_inc(sm.s_qo, 16)
            vo = vh.rearrange("(tt p) d -> tt p d", p=128)
            for tt in range(NTILE):
                pr = tt % 2
                sy.wait_ge(sm.s_ev2, 128 + 2 * (tt + 1))
                sy.dma_start(vo[tt], vstage[pr][:]).then_inc(sm.s_vo, 16)

    if phases < 3:
        emit_dbg()
        return nc

    PH3_LVL = int(os.environ.get("PH3_LVL", "4"))
    PH3_NOEVAC = bool(int(os.environ.get("PH3_NOEVAC", "0")))
    with nc.Block("ph3") as Bl:

        @Bl.gpsimd
        def _(g):
            g.wait_ge(sm.s_qo, 16 * 16)
            g.wait_ge(sm.s_vo, 16 * 32)
            g.dma_start(wout_bf[:].rearrange("p (dc j) -> p dc j", dc=DC),
                        wout_d.rearrange("(dc p) j -> p dc j", p=128)).then_inc(sm.s_wo, 16)
            g.dma_start(w1_buf[:].rearrange("p (dc j) -> p dc j", dc=DC),
                        w1a_d.rearrange("e (dc p) j -> e p dc j", p=128)[0]).then_inc(sm.s_w1, 16)

        @Bl.sync
        def _(sy):
            qi = qh.rearrange("dc p (qb t) -> qb p dc t", qb=NQB)
            ao = ah.rearrange("(qb qq p) d -> qb p qq d", p=128, qq=2)
            store_on = PH3_LVL >= 4 and not PH3_NOEVAC

            def qf_load(qb):
                if qb == 0:
                    sy.wait_ge(sm.s_qo, 16 * 16)
                else:
                    sy.wait_ge(sm.s_sc, 32 * qb)
                sy.dma_start(qf_sb[:].rearrange("p (dc t) -> p dc t", dc=DC),
                             qi[qb]).then_inc(sm.s_qf, 16)

            qf_load(0)
            sy.wait_ge(sm.s_xnfo, 16 * NSLAB)
            vi = vh.rearrange("(g tt p) d -> g p tt d", p=128, tt=8)
            vs = v_sb[:].rearrange("p (g tt d) -> p g tt d", g=4, tt=8)
            for gg in range(4):
                sy.dma_start(vs[:, gg], vi[gg]).then_inc(sm.s_vi, 16)
            for qb in range(NQB):
                if qb + 1 < NQB:
                    qf_load(qb + 1)
                if store_on:
                    sy.wait_ge(sm.s_ave, 4 * (qb + 1))
                    sy.dma_start(ao[qb],
                                 attn_st[:].rearrange("p (qq d) -> p qq d", qq=2)).then_inc(sm.s_ao, 16)

        @Bl.tensor
        def _(t):
            def s_reduce(qb):
                t.wait_ge(sm.s_spp, 32 * (qb + 1))
                if qb >= 1:
                    t.wait_ge(sm.s_siv, qb)
                for qq in range(2):
                    ins = t.matmul(PB[6][:, qq:qq + 1], spp[:, qq * 128:(qq + 1) * 128],
                                   onec_sb[:], start=True, stop=True)
                ins.then_inc(sm.s_ts3, 1)

            def outproj(qb):
                t.wait_ge(sm.s_cve, qb + 1)
                if qb == 0:
                    t.wait_ge(sm.s_wo, 16)
                for qq in range(2):
                    for dc in range(DC):
                        lhs = ctx_sb[:, dc * 256 + qq * 128: dc * 256 + (qq + 1) * 128]
                        t.ldweights(lhs)
                        for h in range(2):
                            ins = t.matmul(PB[2 + qq * 2 + h][:], lhs,
                                           wout_bf[:, dc * 1024 + h * 512: dc * 1024 + (h + 1) * 512],
                                           start=(dc == 0), stop=(dc == DC - 1))
                    ins.then_inc(sm.s_to, 1)

            for qb in range(NQB):
                t.wait_ge(sm.s_qf, 16 * (qb + 1))
                for kc in range(KC):
                    gi = qb * KC + kc
                    if gi >= 2:
                        t.wait_ge(sm.s_exp, gi - 1)
                    slot = kc % 4
                    pbs = PB[slot % 2][:, (slot // 2) * 256:(slot // 2 + 1) * 256]
                    for dc in range(DC):
                        lhs = k_sb[:, dc * 4096 + kc * 128: dc * 4096 + (kc + 1) * 128]
                        t.ldweights(lhs)
                        ins = t.matmul(pbs, lhs, qf_sb[:, dc * QBS:(dc + 1) * QBS],
                                       start=(dc == 0), stop=(dc == DC - 1))
                    ins.then_inc(sm.s_sc, 1)
                    if qb >= 1 and PH3_LVL >= 3 and kc == 4:
                        s_reduce(qb - 1)
                    if qb >= 1 and PH3_LVL >= 4 and kc == 10:
                        outproj(qb - 1)
                if PH3_LVL < 2:
                    continue
                t.wait_ge(sm.s_exp, 32 * (qb + 1))
                if qb == 0:
                    t.wait_ge(sm.s_vi, 64)
                elif PH3_LVL >= 4 and not PH3_NOEVAC:
                    t.wait_ge(sm.s_ave, 4 * qb)
                else:
                    t.wait_ge(sm.s_cve, qb)
                for dc in range(DC):
                    pbc = PB[2 + dc // 2][:, (dc % 2) * 256:(dc % 2 + 1) * 256]
                    for kc in range(KC):
                        lhs = v_sb[:, kc * 1024 + dc * 128: kc * 1024 + (dc + 1) * 128]
                        t.ldweights(lhs)
                        ins = t.matmul(pbc, lhs, probs[:, kc * QBS:(kc + 1) * QBS],
                                       start=(kc == 0), stop=(kc == KC - 1))
                    ins.then_inc(sm.s_cx, 1)
            if PH3_LVL >= 3:
                s_reduce(NQB - 1)
            if PH3_LVL >= 4:
                outproj(NQB - 1)

        @Bl.scalar
        def _(sc):
            for qb in range(NQB):
                for kc in range(KC):
                    gi = qb * KC + kc
                    sc.wait_ge(sm.s_sc, gi + 1)
                    if qb >= 1 and kc == 0:
                        if PH3_LVL >= 2:
                            sc.wait_ge(sm.s_cx, 8 * qb)
                        if PH3_LVL >= 1:
                            sc.wait_ge(sm.s_spp, 32 * qb)
                    slot = kc % 4
                    pbs = PB[slot % 2][:, (slot // 2) * 256:(slot // 2 + 1) * 256]
                    sc.activation(probs[:, kc * QBS:(kc + 1) * QBS], pbs, AF.Exp,
                                  scale=1.0 / 32.0).then_inc(sm.s_exp, 1)

        @Bl.vector
        def _(v):
            for qb in range(NQB):
                for kc in range(KC):
                    gi = qb * KC + kc
                    v.wait_ge(sm.s_exp, gi + 1)
                    if kc == 0:
                        ins = v.tensor_copy(spp[:], probs[:, 0:QBS])
                    else:
                        ins = v.tensor_add(spp[:], spp[:], probs[:, kc * QBS:(kc + 1) * QBS])
                    ins.then_inc(sm.s_spp, 1)
                if PH3_LVL >= 3:
                    v.wait_ge(sm.s_ts3, qb + 1)
                    v.reciprocal(s_inv[:, 0:2], PB[6][:, 0:2])
                    v.drain().then_inc(sm.s_siv, 1)
                if PH3_LVL >= 2:
                    v.wait_ge(sm.s_cx, 8 * (qb + 1))
                    for b in range(4):
                        ins = v.tensor_copy(ctx_sb[:, b * 512:(b + 1) * 512], PB[2 + b][:])
                    ins.then_inc(sm.s_cve, 1)
                if PH3_LVL < 4 or PH3_NOEVAC:
                    continue
                if qb >= 1:
                    v.wait_ge(sm.s_ao, 16 * qb)
                for qq in range(2):
                    v.wait_ge(sm.s_to, qb * 2 + qq + 1)
                    for h in range(2):
                        if os.environ.get("PH3_EVAC", "stt") == "copy":
                            ins = v.tensor_copy(
                                attn_st[:, qq * 1024 + h * 512: qq * 1024 + (h + 1) * 512],
                                PB[2 + qq * 2 + h][:])
                        else:
                            ins = v.scalar_tensor_tensor(
                                attn_st[:, qq * 1024 + h * 512: qq * 1024 + (h + 1) * 512],
                                PB[2 + qq * 2 + h][:], s_inv[:, qq:qq + 1],
                                bo_bc[:, h * 512:(h + 1) * 512], ALU.mult, ALU.add)
                    ins.then_inc(sm.s_ave, 2)

    if phases < 4:
        emit_dbg()
        return nc

    def m1_bank_wait(t, idx, gidx, tbg):
        if idx == 0 and gidx == 0:
            return
        if gidx == 0:
            t.wait_ge(sm.s_ge, (idx - 1) * 12 + 10 + tbg + 1)
        elif gidx % 2 == 0:
            t.wait_ge(sm.s_ge, idx * 12 + gidx - 2 + tbg + 1)
        else:
            t.wait_ge(sm.s_ye, idx * 12 + gidx - 1 + tbg + 1)

    with nc.Block("ph4") as Bl:

        @Bl.gpsimd
        def _(g):
            w1v = w1a_d.rearrange("e (dc p) j -> e p dc j", p=128)
            w2v = w2a_d.rearrange("e (hc p) j -> e p hc j", p=128)
            g.dma_start(w2_buf[:].rearrange("p (hc j) -> p hc j", hc=HC2),
                        w2v[0]).then_inc(sm.s_w2, 16)
            for idx in range(1, 2 * NE):
                e = idx % NE
                g.wait_ge(sm.s_m1f, idx)
                g.dma_start(w1_buf[:].rearrange("p (dc j) -> p dc j", dc=DC),
                            w1v[e]).then_inc(sm.s_w1, 16)
                g.wait_ge(sm.s_m2f, idx)
                g.dma_start(w2_buf[:].rearrange("p (hc j) -> p hc j", hc=HC2),
                            w2v[e]).then_inc(sm.s_w2, 16)

        @Bl.sync
        def _(sy):
            xs = x_d.rearrange("(tt p) d -> tt p d", p=128)
            ai = ah.rearrange("(tt p) d -> tt p d", p=128)
            oo = out_d.rearrange("(tt p) d -> tt p d", p=128)
            xf = xfh.rearrange("dc p (hf t) -> hf p dc t", hf=2)
            sy.wait_ge(sm.s_ao, 16 * NQB)

            def prefetch(gt):
                if gt >= 2:
                    sy.wait_ge(sm.s_cmb, gt - 1)
                sy.dma_start(x_re[gt % 2][:], xs[gt]).then_inc(sm.s_x2, 16)
                sy.dma_start(attn_re[gt % 2][:], ai[gt]).then_inc(sm.s_ai, 16)

            for half in range(2):
                if half == 1:
                    sy.wait_ge(sm.s_m1f, NE)
                sy.dma_start(xh_sb[:].rearrange("p (dc t) -> p dc t", dc=DC),
                             xf[half]).then_inc(sm.s_xh, 16)
                prefetch(half * TTH)
                prefetch(half * TTH + 1)
                for tt in range(TTH):
                    gt = half * TTH + tt
                    sy.wait_ge(sm.s_cmb, gt + 1)
                    sy.dma_start(oo[gt], out_st[:]).then_inc(sm.s_out, 16)
                    if tt + 2 < TTH:
                        prefetch(gt + 2)
            sy.wait_ge(sm.s_out, 16 * 32)

        @Bl.tensor
        def _(t):
            for half in range(2):
                for e in range(NE):
                    idx = half * NE + e
                    t.wait_ge(sm.s_w1, 16 * (idx + 1))
                    t.wait_ge(sm.s_xh, 16 * (half + 1))
                    for gidx, hc in enumerate(HCORD):
                        for tbg in range(2):
                            gg = gidx * 2 + tbg
                            m1_bank_wait(t, idx, gidx, tbg)
                            for dc in range(DC):
                                lhs = w1_buf[:, dc * 1536 + hc * 128: dc * 1536 + (hc + 1) * 128]
                                t.ldweights(lhs)
                                for tb in range(2):
                                    ins = t.matmul(PB[tbg * 2 + tb][:], lhs,
                                                   xh_sb[:, dc * 2048 + tbg * 1024 + tb * 512:
                                                         dc * 2048 + tbg * 1024 + (tb + 1) * 512],
                                                   start=(dc == 0), stop=(dc == DC - 1))
                            if gg == 23:
                                ins.then_inc(sm.s_m1g, 1)
                                t.nop().then_inc(sm.s_m1f, 1)
                            else:
                                ins.then_inc(sm.s_m1g, 1)
                    t.wait_ge(sm.s_w2, 16 * (idx + 1))
                    t.wait_ge(sm.s_act, idx * HC2 + HC2)
                    for tt in range(TTH):
                        zi = idx * TTH + tt
                        if zi >= 2:
                            t.wait_ge(sm.s_zadd, zi - 1)
                        for hc in range(HC2):
                            lhs = y_sb[:, hc * 2048 + tt * 128: hc * 2048 + (tt + 1) * 128]
                            t.ldweights(lhs)
                            for h in range(2):
                                ins = t.matmul(PB[4 + (tt % 2) * 2 + h][:], lhs,
                                               w2_buf[:, hc * 1024 + h * 512: hc * 1024 + (h + 1) * 512],
                                               start=(hc == 0), stop=(hc == HC2 - 1))
                        if tt == TTH - 1:
                            ins.then_inc(sm.s_m2t, 1)
                            t.nop().then_inc(sm.s_m2f, 1)
                        else:
                            ins.then_inc(sm.s_m2t, 1)

        @Bl.scalar
        def _(sc):
            for half in range(2):
                for e in range(NE):
                    idx = half * NE + e
                    for gidx, hc in enumerate(HCORD):
                        if hc < HC2:
                            continue
                        for tbg in range(2):
                            gg = gidx * 2 + tbg
                            sc.wait_ge(sm.s_m1g, idx * 24 + gg + 1)
                            for tb in range(2):
                                ins = sc.activation(
                                    sg_sb[:, (hc - HC2) * 2048 + tbg * 1024 + tb * 512:
                                          (hc - HC2) * 2048 + tbg * 1024 + (tb + 1) * 512],
                                    PB[tbg * 2 + tb][:], AF.Silu)
                            ins.then_inc(sm.s_ge, 1)

        @Bl.vector
        def _(v):
            for half in range(2):
                for e in range(NE):
                    idx = half * NE + e
                    for gidx, hc in enumerate(HCORD):
                        if hc >= HC2:
                            continue
                        for tbg in range(2):
                            gg = gidx * 2 + tbg
                            v.wait_ge(sm.s_m1g, idx * 24 + gg + 1)
                            for tb in range(2):
                                ins = v.tensor_copy(
                                    y_sb[:, hc * 2048 + tbg * 1024 + tb * 512:
                                         hc * 2048 + tbg * 1024 + (tb + 1) * 512],
                                    PB[tbg * 2 + tb][:])
                            ins.then_inc(sm.s_ye, 1)
                        v.wait_ge(sm.s_ge, idx * 12 + (hc + 1) * 2)
                        v.tensor_mul(y_sb[:, hc * 2048:(hc + 1) * 2048],
                                     y_sb[:, hc * 2048:(hc + 1) * 2048],
                                     sg_sb[:, hc * 2048:(hc + 1) * 2048]).then_inc(sm.s_act, 1)
                    for tt in range(TTH):
                        zi = idx * TTH + tt
                        gt = half * TTH + tt
                        v.wait_ge(sm.s_m2t, zi + 1)
                        wcol = w_all[:, gt * 8 + min(e, 7): gt * 8 + min(e, 7) + 1]
                        for h in range(2):
                            dst = z_sb[:, tt * 1024 + h * 512: tt * 1024 + (h + 1) * 512]
                            pbm = PB[4 + (tt % 2) * 2 + h][:]
                            if e == 0:
                                ins = v.tensor_scalar(dst, pbm, wcol, None, ALU.mult)
                            elif e < 8:
                                ins = v.scalar_tensor_tensor(dst, pbm, wcol, dst, ALU.mult, ALU.add)
                            else:
                                ins = v.tensor_tensor(dst, pbm, dst, ALU.add)
                        ins.then_inc(sm.s_zadd, 1)
                for tt in range(TTH):
                    gt = half * TTH + tt
                    v.wait_ge(sm.s_x2, 16 * (gt + 1))
                    v.wait_ge(sm.s_ai, 16 * (gt + 1))
                    if gt > 0:
                        v.wait_ge(sm.s_out, 16 * gt)
                    v.tensor_add(out_st[:], x_re[gt % 2][:], z_sb[:, tt * 1024:(tt + 1) * 1024])
                    v.tensor_add(out_st[:], out_st[:], attn_re[gt % 2][:]).then_inc(sm.s_cmb, 1)

    emit_dbg()
    return nc


_NC_CACHE = {}


def _get_nc(phases=4, debug=False):
    key = (phases, debug)
    if key not in _NC_CACHE:
        _NC_CACHE[key] = build(phases, debug)
    return _NC_CACHE[key]


def prep_inputs(L0_attn_x, L0_attn_W_qkv, L0_attn_b_qkv, L0_attn_W_out, L0_attn_b_out,
                L0_ffn_gate_w, L0_ffn_W1, L0_ffn_W2, L0_ffn_Ws1, L0_ffn_Ws2):
    f = np.float32
    x = np.ascontiguousarray(np.asarray(L0_attn_x, dtype=f))
    wqkv = np.ascontiguousarray(np.asarray(L0_attn_W_qkv, dtype=f))
    bqkv = np.asarray(L0_attn_b_qkv, dtype=f)
    bqk = np.ascontiguousarray(bqkv[:2048].reshape(16, 128).T)
    bv = np.ascontiguousarray(bqkv[2048:].reshape(1, 1024))
    wout = np.ascontiguousarray(np.asarray(L0_attn_W_out, dtype=f))
    bo = np.ascontiguousarray(np.asarray(L0_attn_b_out, dtype=f).reshape(1, 1024))
    gwt = np.ascontiguousarray(np.asarray(L0_ffn_gate_w, dtype=f).T)
    W1 = np.asarray(L0_ffn_W1, dtype=f)
    W2 = np.asarray(L0_ffn_W2, dtype=f)
    Ws1 = np.asarray(L0_ffn_Ws1, dtype=f)
    Ws2 = np.asarray(L0_ffn_Ws2, dtype=f)

    w1a = np.empty((NE, D, 2 * HEXP), dtype=f)
    w2a = np.empty((NE, HEXP, D), dtype=f)
    for e in range(8):
        w1a[e] = W1[e].T
        w2a[e] = W2[e].T
    ws1t = Ws1.T
    ws2t = Ws2.T
    w1a[8] = np.concatenate([ws1t[:, 0:768], ws1t[:, 1536:2304]], axis=1)
    w1a[9] = np.concatenate([ws1t[:, 768:1536], ws1t[:, 2304:3072]], axis=1)
    w2a[8] = ws2t[0:768]
    w2a[9] = ws2t[768:1536]
    w1a = np.ascontiguousarray(w1a)
    w2a = np.ascontiguousarray(w2a)

    shared = {"wqkv": wqkv, "bqk": bqk, "bv": bv, "bo": bo, "wout": wout,
              "gwt": gwt, "w1a": w1a, "w2a": w2a}
    in_maps = []
    for c in range(8):
        m = {"x": np.ascontiguousarray(x[c])}
        m.update(shared)
        in_maps.append(m)
    return in_maps


def kernel(**inputs):
    in_maps = prep_inputs(**inputs)
    nc = _get_nc()
    res = run_bass_kernel_spmd(nc, in_maps, core_ids=list(range(8)))
    return np.stack([r["out"] for r in res.results], axis=0)


# version 04
# speedup vs baseline: 1.1055x; 1.0488x over previous
import os
import numpy as np
import ml_dtypes

import concourse.bass as bass
import concourse.mybir as mybir
from concourse.bass_utils import run_bass_kernel_spmd

F32 = mybir.dt.float32
BF16 = mybir.dt.bfloat16
AF = mybir.ActivationFunctionType
ALU = mybir.AluOpType
AX = mybir.AxisListType
bfdt = ml_dtypes.bfloat16

S = 4096
D = 1024
DC = 8
NTILE = 32
NSLAB = 8
QBS = 256
NQB = S // QBS
KC = 32
NE = 10
HEXP = 768
HC1 = 12
HC2 = 6
NHALF = 2048
TTH = 16

BASE = 16512
HCORD = [0, 6, 1, 7, 2, 8, 3, 9, 4, 10, 5, 11]


def build(phases=4, debug=False):
    nc = bass.Bass("TRN2", target_bir_lowering=False)

    x_d = nc.dram_tensor("x", [S, D], F32, kind="ExternalInput").ap()
    wqkv_d = nc.dram_tensor("wqkv", [D, 3 * D], F32, kind="ExternalInput").ap()
    bqk_d = nc.dram_tensor("bqk", [128, 16], F32, kind="ExternalInput").ap()
    bv_d = nc.dram_tensor("bv", [1, D], F32, kind="ExternalInput").ap()
    bo_d = nc.dram_tensor("bo", [1, D], F32, kind="ExternalInput").ap()
    wout_d = nc.dram_tensor("wout", [D, D], F32, kind="ExternalInput").ap()
    gwt_d = nc.dram_tensor("gwt", [D, 8], F32, kind="ExternalInput").ap()
    w1a_d = nc.dram_tensor("w1a", [NE, D, 2 * HEXP], F32, kind="ExternalInput").ap()
    w2a_d = nc.dram_tensor("w2a", [NE, HEXP, D], F32, kind="ExternalInput").ap()
    out_d = nc.dram_tensor("out", [S, D], F32, kind="ExternalOutput").ap()

    dbg_kind = "ExternalOutput" if debug else "Internal"
    qh = nc.dram_tensor("q_hbm", [DC, 128, S], BF16, kind=dbg_kind).ap()
    vh = nc.dram_tensor("v_hbm", [S, D], BF16, kind=dbg_kind).ap()
    ah = nc.dram_tensor("attn_hbm", [S, D], BF16, kind=dbg_kind).ap()
    xfh = nc.dram_tensor("xnf_hbm", [DC, 128, S], BF16, kind=dbg_kind).ap()
    if debug:
        dbgw_d = nc.dram_tensor("dbg_w", [128, NTILE * 8], F32, kind="ExternalOutput").ap()

    idb = nc.inline_tensor(np.eye(128, dtype=bfdt), name="ident_bf").ap()
    i8f = nc.inline_tensor(np.eye(8, dtype=np.float32), name="ident8_f").ap()
    onesc = nc.inline_tensor(np.ones((128, 1), dtype=np.float32), name="ones_col").ap()
    onesr = nc.inline_tensor(np.ones((1, 128), dtype=np.float32), name="ones_row").ap()

    def at(name, shape, dtype, off):
        assert off >= BASE, name
        nbytes = int(np.prod(shape[1:])) * (2 if dtype == BF16 else 4)
        assert off + nbytes <= 229344, (name, off, nbytes)
        return nc.alloc_sbuf_tensor_at(name, shape, dtype, offset=off)

    A = BASE
    B = BASE + 65536
    C = BASE + 131072
    DD = BASE + 180224

    xnf = at("xnf", [128, DC * S], BF16, A)
    v_sb = at("v_sb", [128, NTILE * D], BF16, A)
    z_sb = at("z_sb", [128, TTH * D], F32, A)

    x_buf = [at(f"x_buf{i}", [128, 4 * D], F32, B + i * 16384) for i in range(2)]
    xn32 = [at(f"xn32_{i}", [128, D], F32, B + 32768 + i * 4096) for i in range(2)]
    hi_tok = [at(f"hi{i}", [128, D], BF16, B + 40960 + i * 2048) for i in range(2)]
    lo_tok = [at(f"lo{i}", [128, D], BF16, B + 45056 + i * 2048) for i in range(2)]
    lo_f = at("lof", [128, DC * 512], BF16, B + 49152)
    k_sb = at("k_sb", [128, DC * S], BF16, B)
    y_sb = at("y_sb", [128, HC2 * NHALF], BF16, B)
    sg_sb = at("sg_sb", [128, HC2 * NHALF], BF16, B + 24576)
    w2_buf = at("w2_buf", [128, HC2 * D], BF16, B + 49152)
    brow = at("brow", [1, D], F32, B + 57344)
    lgT = at("lgT", [8, 512], F32, B + 61440)
    junk = at("junk", [128, D], BF16, B + 63488)
    probs_all = at("probs_all", [128, NTILE * 8], F32, B + 63488)
    msk_all = at("msk_all", [128, NTILE * 8], F32, B + 57344)
    l2_all = at("l2_all", [128, NTILE * 8], F32, B + 58368)

    wqkv_bf = at("wqkv_bf", [128, DC * 3 * D], BF16, C)
    wout_bf = at("wout_bf", [128, DC * D], BF16, C)
    qf_sb = at("qf_sb", [128, DC * QBS], BF16, C + 16384)
    probs = at("probs", [128, KC * QBS], BF16, C + 20480)
    spp = at("spp", [128, QBS], F32, C + 36864)
    ctx_sb = at("ctx_sb", [128, DC * QBS], BF16, C + 37888)
    attn_st = at("attn_st", [128, 2 * D], BF16, C + 41984)
    xh_sb = at("xh_sb", [128, DC * NHALF], BF16, C)
    x_re = [at(f"x_re{i}", [128, D], F32, C + 32768 + i * 4096) for i in range(2)]
    attn_re = [at(f"attn_re{i}", [128, D], BF16, C + 40960 + i * 2048) for i in range(2)]
    out_st = [at("out_st0", [128, D], F32, C + 45056),
              at("out_st1", [128, D], F32, B + 61440)]

    w1_buf = at("w1_buf", [128, DC * 2 * HEXP], BF16, DD)
    qstage = [at(f"qstage{i}", [128, 2048], BF16, DD + i * 4096) for i in range(2)]
    vstage = [at(f"vstage{i}", [128, D], BF16, DD + 8192 + i * 2048) for i in range(2)]
    o = DD + 24576
    w_all = at("w_all", [128, NTILE * 8], F32, o); o += 1024
    gwt_f = at("gwt_f", [128, DC * 8], F32, o); o += 256
    gw_hi = at("gw_hi", [128, DC * 8], BF16, o); o += 128
    gw_lo = at("gw_lo", [128, DC * 8], BF16, o); o += 128
    b_sb = at("b_sb", [128, 16], F32, o); o += 64
    id_sb = at("id_sb", [128, 128], BF16, o); o += 256
    i8_sb = at("i8_sb", [8, 8], F32, o); o += 32
    onec_sb = at("onec_sb", [128, 1], F32, o); o += 32
    oner_sb = at("oner_sb", [1, 128], F32, o); o += 512
    lg_tok = at("lg_tok", [128, NTILE * 8], F32, o); o += 1024
    stt = at("stt", [128, 64], F32, o); o += 256
    bv_bc = at("bv_bc", [128, D], BF16, o); o += 2048
    bo_bc = at("bo_bc", [128, D], BF16, o); o += 2048
    s_inv = at("s_inv", [128, 4], F32, o); o += 32
    assert o <= 229344, o

    PB = [nc.alloc_psum_tensor(f"pb{i}", [128, 512], F32) for i in range(8)]

    names = [
        "s_cst", "s_x1", "s_wq", "s_xnfo", "s_qo", "s_vo", "s_vi", "s_qf", "s_ao",
        "s_wo", "s_w1", "s_w2", "s_xh", "s_x2", "s_ai", "s_out", "s_bvbo",
        "s_v1", "s_vu", "s_c1", "s_vhi", "s_th", "s_tl", "s_v1e", "s_vloe",
        "s_tg", "s_vlg", "s_tk", "s_vtok", "s_tb", "s_vgw", "s_gsm", "s_c1a",
        "s_ev2", "s_t2",
        "s_sc", "s_exp", "s_spp", "s_cx", "s_cve", "s_ts3", "s_siv", "s_to", "s_ave",
        "s_m1g", "s_ye", "s_ge", "s_m1f", "s_act", "s_m2t", "s_m2f", "s_zadd", "s_cmb",
    ]
    sm = type("SM", (), {n: nc.alloc_semaphore(n) for n in names})


    def emit_dbg():
        if not debug:
            return
        with nc.Block("dbg") as Bl:
            @Bl.sync
            def _(sy):
                sy.dma_start(dbgw_d, w_all[:]).then_inc(sm.s_cst, 16)
                sy.wait_ge(sm.s_cst, 16 * 7)

    def xnf3(apx):
        return apx.rearrange("p (dc t) -> p dc t", dc=DC)

    with nc.Block("ph1") as Bl:

        @Bl.sync
        def _(sy):
            sy.dma_start(id_sb[:], idb).then_inc(sm.s_cst, 16)
            sy.dma_start(i8_sb[:], i8f).then_inc(sm.s_cst, 16)
            sy.dma_start(onec_sb[:], onesc).then_inc(sm.s_cst, 16)
            sy.dma_start(oner_sb[:], onesr).then_inc(sm.s_cst, 16)
            sy.dma_start(gwt_f[:].rearrange("p (dc e) -> p dc e", dc=DC),
                         gwt_d.rearrange("(dc p) e -> p dc e", p=128)).then_inc(sm.s_cst, 16)
            sy.dma_start(b_sb[:], bqk_d).then_inc(sm.s_cst, 16)
            xs = x_d.rearrange("(s q p) d -> s p q d", p=128, q=4)
            for s in range(NSLAB):
                if s >= 2:
                    sy.wait_ge(sm.s_v1, 4 * s - 4)
                sy.dma_start(x_buf[s % 2][:].rearrange("p (q d) -> p q d", q=4),
                             xs[s]).then_inc(sm.s_x1, 16)
            xo = xfh.rearrange("dc p (sl t) -> sl p dc t", sl=NSLAB)
            for s in range(NSLAB):
                sy.wait_ge(sm.s_v1e, 4 * s + 4)
                sy.dma_start(xo[s],
                             xnf3(xnf[:])[:, :, s * 512:(s + 1) * 512]).then_inc(sm.s_xnfo, 16)
            sy.dma_start(brow[:], bv_d).then_inc(sm.s_bvbo, 16)
            sy.wait_ge(sm.s_tb, 1)
            sy.dma_start(brow[:], bo_d).then_inc(sm.s_bvbo, 16)

        @Bl.gpsimd
        def _(g):
            g.dma_start(wqkv_bf[:].rearrange("p (dc j) -> p dc j", dc=DC),
                        wqkv_d.rearrange("(dc p) j -> p dc j", p=128)).then_inc(sm.s_wq, 16)

        @Bl.vector
        def _(v):
            v.wait_ge(sm.s_cst, 16 * 6)
            v.tensor_copy(gw_hi[:], gwt_f[:])
            v.tensor_sub(gw_lo[:], gwt_f[:], gw_hi[:])
            v.drain().then_inc(sm.s_vgw, 1)
            def stats1(sl):
                c0 = (sl % 2) * 32
                v.wait_ge(sm.s_x1, 16 * (sl + 1))
                if sl >= 2:
                    v.wait_ge(sm.s_c1a, sl - 1)
                for q in range(4):
                    xcol = x_buf[sl % 2][:, q * D:(q + 1) * D]
                    v.reduce_sum(stt[:, c0 + q:c0 + q + 1], xcol, AX.X)
                    v.scalar_tensor_tensor(junk[:], xcol, 1.0, xcol, ALU.mult, ALU.mult,
                                           accum_out=stt[:, c0 + 4 + q:c0 + 5 + q])
                v.drain()
                v.tensor_scalar(stt[:, c0 + 8:c0 + 12], stt[:, c0:c0 + 4], 1.0 / D, None, ALU.mult)
                v.tensor_scalar(stt[:, c0 + 12:c0 + 16], stt[:, c0 + 4:c0 + 8], 1.0 / D, 1e-5,
                                ALU.mult, ALU.add)
                v.drain()
                v.tensor_mul(stt[:, c0 + 16:c0 + 20], stt[:, c0 + 8:c0 + 12], stt[:, c0 + 8:c0 + 12])
                v.drain()
                v.scalar_tensor_tensor(stt[:, c0 + 12:c0 + 16], stt[:, c0 + 16:c0 + 20], -1.0,
                                       stt[:, c0 + 12:c0 + 16], ALU.mult, ALU.add)
                v.drain()
                v.reciprocal(stt[:, c0 + 16:c0 + 20], stt[:, c0 + 12:c0 + 16])
                v.drain().then_inc(sm.s_vu, 1)

            def newton(sl):
                c0 = (sl % 2) * 32
                v.wait_ge(sm.s_c1a, sl + 1)
                v.tensor_mul(stt[:, c0 + 24:c0 + 28], stt[:, c0 + 20:c0 + 24], stt[:, c0 + 20:c0 + 24])
                v.drain()
                v.tensor_mul(stt[:, c0 + 24:c0 + 28], stt[:, c0 + 24:c0 + 28], stt[:, c0 + 12:c0 + 16])
                v.drain()
                v.tensor_scalar(stt[:, c0 + 24:c0 + 28], stt[:, c0 + 24:c0 + 28], -0.5, 1.5,
                                ALU.mult, ALU.add)
                v.drain()
                v.tensor_mul(stt[:, c0 + 24:c0 + 28], stt[:, c0 + 24:c0 + 28], stt[:, c0 + 20:c0 + 24])
                v.drain()

            stats1(0)
            newton(0)
            for sl in range(NSLAB):
                c0 = (sl % 2) * 32
                for q in range(4):
                    ti = sl * 4 + q
                    if q == 1 and sl + 1 < NSLAB:
                        stats1(sl + 1)
                    p = ti % 2
                    xcol = x_buf[sl % 2][:, q * D:(q + 1) * D]
                    if ti >= 2:
                        v.wait_ge(sm.s_c1, ti - 1)
                    v.tensor_scalar(xn32[p][:], xcol, stt[:, c0 + 8 + q:c0 + 9 + q],
                                    stt[:, c0 + 24 + q:c0 + 25 + q],
                                    ALU.subtract, ALU.mult)
                    v.drain().then_inc(sm.s_v1, 1)
                    v.wait_ge(sm.s_c1, ti + 1)
                    if ti >= 2:
                        v.wait_ge(sm.s_tl, ti - 1)
                    v.tensor_sub(lo_tok[p][:], xn32[p][:], hi_tok[p][:])
                    v.drain().then_inc(sm.s_vhi, 1)
                v.wait_ge(sm.s_tg, sl + 1)
                v.tensor_copy(lgT[:], PB[0][0:8, :])
                v.drain().then_inc(sm.s_vlg, 1)
                v.wait_ge(sm.s_tk, sl + 1)
                v.tensor_copy(lg_tok[:, sl * 32:(sl + 1) * 32], PB[1][:, 0:32])
                v.drain().then_inc(sm.s_vtok, 1)
                if sl + 1 < NSLAB:
                    newton(sl + 1)
            v.wait_ge(sm.s_tb, 1)
            v.tensor_copy(bv_bc[:, 0:512], PB[0][:])
            v.tensor_copy(bv_bc[:, 512:1024], PB[1][:])
            v.wait_ge(sm.s_tb, 4)
            v.tensor_copy(bo_bc[:, 0:512], PB[2][:])
            v.tensor_copy(bo_bc[:, 512:1024], PB[3][:])
            v.wait_ge(sm.s_gsm, 1)
            lg3 = lg_tok[:].rearrange("p (g k) -> p g k", g=NTILE)
            pr3 = probs_all[:].rearrange("p (g k) -> p g k", g=NTILE)
            mk3 = msk_all[:].rearrange("p (g k) -> p g k", g=NTILE)
            l23 = l2_all[:].rearrange("p (g k) -> p g k", g=NTILE)
            wa3 = w_all[:].rearrange("p (g k) -> p g k", g=NTILE)

            def bc(col_ap):
                a3 = col_ap.rearrange("p (g o) -> p g o", o=1)
                import concourse.ap as _cap
                return _cap.AP(a3.tensor, a3.offset, [a3.ap[0], a3.ap[1], [0, 8]])

            v.reduce_sum(stt[:, 0:NTILE], pr3, AX.X)
            v.drain()
            v.reciprocal(stt[:, 32:32 + NTILE], stt[:, 0:NTILE])
            v.reduce_max(stt[:, 0:NTILE], lg3, AX.X)
            v.drain()
            v.tensor_tensor(mk3, lg3, bc(stt[:, 0:NTILE]), ALU.is_ge)
            v.drain()
            v.scalar_tensor_tensor(l2_all[:], msk_all[:], -1e30, lg_tok[:], ALU.mult, ALU.add)
            v.drain()
            v.reduce_max(stt[:, 0:NTILE], l23, AX.X)
            v.drain()
            v.tensor_tensor(mk3, lg3, bc(stt[:, 0:NTILE]), ALU.is_ge)
            v.drain()
            v.tensor_mul(msk_all[:], msk_all[:], probs_all[:])
            v.drain()
            v.tensor_tensor(wa3, mk3, bc(stt[:, 32:32 + NTILE]), ALU.mult)
            v.drain().then_inc(sm.s_v1, 1)

        @Bl.scalar
        def _(sc):
            def do_hi(tj):
                pj = tj % 2
                sc.wait_ge(sm.s_v1, tj + 1)
                sc.copy(hi_tok[pj][:], xn32[pj][:])
                sc.drain().then_inc(sm.s_c1, 1)

            sc.wait_ge(sm.s_vu, 1)
            sc.sqrt(stt[:, 20:24], stt[:, 16:20])
            sc.drain().then_inc(sm.s_c1a, 1)
            do_hi(0)
            for ti in range(NTILE):
                sl, q = divmod(ti, 4)
                p = ti % 2
                if q == 1 and sl + 1 < NSLAB:
                    c1 = ((sl + 1) % 2) * 32
                    sc.wait_ge(sm.s_vu, sl + 2)
                    sc.sqrt(stt[:, c1 + 20:c1 + 24], stt[:, c1 + 16:c1 + 20])
                    sc.drain().then_inc(sm.s_c1a, 1)
                if ti + 1 < NTILE and q != 3:
                    do_hi(ti + 1)
                sc.wait_ge(sm.s_th, ti + 1)
                dsth = xnf3(xnf[:])[:, :, ti * 128:(ti + 1) * 128]
                sc.copy(dsth[:, 0:4, :], PB[p * 2][:].rearrange("p (a t) -> p a t", a=4))
                sc.copy(dsth[:, 4:8, :],
                        PB[p * 2 + 1][:].rearrange("p (a t) -> p a t", a=4)).then_inc(sm.s_v1e, 1)
                sc.wait_ge(sm.s_tl, ti + 1)
                if ti >= 4:
                    sc.wait_ge(sm.s_tg, ti // 4)
                dst = lo_f[:].rearrange("p (dc t) -> p dc t", dc=DC)[:, :, q * 128:(q + 1) * 128]
                sc.copy(dst[:, 0:4, :], PB[4 + p * 2][:].rearrange("p (a t) -> p a t", a=4))
                sc.copy(dst[:, 4:8, :],
                        PB[4 + p * 2 + 1][:].rearrange("p (a t) -> p a t", a=4)).then_inc(sm.s_vloe, 1)
                if ti + 1 < NTILE and q == 3:
                    do_hi(ti + 1)
            sc.wait_ge(sm.s_vtok, NSLAB)
            sc.wait_ge(sm.s_tg, NSLAB)
            sc.wait_ge(sm.s_tb, 4)
            sc.activation(probs_all[:], lg_tok[:], AF.Exp)
            sc.drain().then_inc(sm.s_gsm, 1)

        @Bl.tensor
        def _(t):
            t.wait_ge(sm.s_cst, 16 * 6)
            for ti in range(NTILE):
                p = ti % 2
                s, q = divmod(ti, 4)
                t.wait_ge(sm.s_c1, ti + 1)
                if ti >= 2:
                    t.wait_ge(sm.s_v1e, ti - 1)
                if q == 0 and s >= 1 and p == 0:
                    t.wait_ge(sm.s_vlg, s)
                    t.wait_ge(sm.s_vtok, s)
                for dc in range(DC):
                    lhs = hi_tok[p][:, dc * 128:(dc + 1) * 128]
                    t.ldweights(lhs)
                    ins = t.matmul(PB[p * 2 + dc // 4][:, (dc % 4) * 128:(dc % 4 + 1) * 128],
                                   lhs, id_sb[:], start=True, stop=True)
                ins.then_inc(sm.s_th, 1)
                t.wait_ge(sm.s_vhi, ti + 1)
                if ti >= 2:
                    t.wait_ge(sm.s_vloe, ti - 1)
                for dc in range(DC):
                    lhs = lo_tok[p][:, dc * 128:(dc + 1) * 128]
                    t.ldweights(lhs)
                    ins = t.matmul(PB[4 + p * 2 + dc // 4][:, (dc % 4) * 128:(dc % 4 + 1) * 128],
                                   lhs, id_sb[:], start=True, stop=True)
                ins.then_inc(sm.s_tl, 1)
                if q == 3:
                    t.wait_ge(sm.s_vgw, 1)
                    t.wait_ge(sm.s_v1e, 4 * s + 4)
                    t.wait_ge(sm.s_vloe, 4 * s + 4)
                    if s >= 1:
                        t.wait_ge(sm.s_vlg, s)
                    nmm = 0
                    for (gw, mvsrc) in ((gw_hi, "hi"), (gw_lo, "hi"), (gw_hi, "lo")):
                        for dc in range(DC):
                            t.ldweights(gw[:, dc * 8:(dc + 1) * 8])
                            if mvsrc == "hi":
                                mv = xnf3(xnf[:])[:, dc, s * 512:(s + 1) * 512]
                            else:
                                mv = lo_f[:, dc * 512:(dc + 1) * 512]
                            ins = t.matmul(PB[0][0:8, :], gw[:, dc * 8:(dc + 1) * 8], mv,
                                           start=(nmm == 0), stop=(nmm == 23))
                            nmm += 1
                    ins.then_inc(sm.s_tg, 1)
                    t.wait_ge(sm.s_vlg, s + 1)
                    if s >= 1:
                        t.wait_ge(sm.s_vtok, s)
                    for qq in range(4):
                        ins = t.matmul(PB[1][:, qq * 8:(qq + 1) * 8],
                                       lgT[0:8, qq * 128:(qq + 1) * 128], i8_sb[:],
                                       start=True, stop=True)
                    ins.then_inc(sm.s_tk, 1)
            t.wait_ge(sm.s_bvbo, 16)
            t.wait_ge(sm.s_v1e, NTILE)
            t.wait_ge(sm.s_vlg, NSLAB)
            t.wait_ge(sm.s_vtok, NSLAB)
            t.matmul(PB[0][:], oner_sb[:], brow[:, 0:512], start=True, stop=True)
            t.matmul(PB[1][:], oner_sb[:], brow[:, 512:1024], start=True, stop=True).then_inc(sm.s_tb, 1)
            t.wait_ge(sm.s_bvbo, 32)
            t.wait_ge(sm.s_vloe, NTILE)
            t.matmul(PB[2][:], oner_sb[:], brow[:, 0:512], start=True, stop=True)
            t.matmul(PB[3][:], oner_sb[:], brow[:, 512:1024], start=True, stop=True).then_inc(sm.s_tb, 3)

    if phases < 2:
        emit_dbg()
        return nc

    with nc.Block("ph2") as Bl:

        @Bl.tensor
        def _(t):
            t.wait_ge(sm.s_wq, 16)
            for u in range(32):
                oc, tbg = divmod(u, 2)
                quad = u % 2
                if u >= 2:
                    t.wait_ge(sm.s_ev2, 4 * (u - 1))
                for dc in range(DC):
                    lhs = wqkv_bf[:, dc * 3072 + oc * 128: dc * 3072 + (oc + 1) * 128]
                    t.ldweights(lhs)
                    for tb in range(4):
                        ins = t.matmul(PB[quad * 4 + tb][:], lhs,
                                       xnf3(xnf[:])[:, dc, (tbg * 4 + tb) * 512:(tbg * 4 + tb + 1) * 512],
                                       start=(dc == 0), stop=(dc == DC - 1))
                ins.then_inc(sm.s_t2, 1)
            for tt in range(NTILE):
                pr = tt % 2
                if tt < 2:
                    t.wait_ge(sm.s_ev2, 124)
                else:
                    t.wait_ge(sm.s_ev2, 128 + 2 * (tt - 1))
                for dc in range(DC):
                    lhs = xnf3(xnf[:])[:, dc, tt * 128:(tt + 1) * 128]
                    t.ldweights(lhs)
                    for h in range(2):
                        ins = t.matmul(PB[pr * 2 + h][:], lhs,
                                       wqkv_bf[:, dc * 3072 + 2048 + h * 512: dc * 3072 + 2048 + (h + 1) * 512],
                                       start=(dc == 0), stop=(dc == DC - 1))
                ins.then_inc(sm.s_t2, 1)

        @Bl.scalar
        def _(sc):
            for u in range(32):
                oc, tbg = divmod(u, 2)
                quad = u % 2
                sc.wait_ge(sm.s_t2, u + 1)
                if oc < 8 and u >= 2:
                    sc.wait_ge(sm.s_qo, 16 * (u - 1))
                for tb in range(4):
                    tok = (tbg * 4 + tb) * 512
                    if oc < 8:
                        dst = qstage[u % 2][:, tb * 512:(tb + 1) * 512]
                    else:
                        dst = k_sb[:, (oc - 8) * 4096 + tok:(oc - 8) * 4096 + tok + 512]
                    ins = sc.activation(dst, PB[quad * 4 + tb][:], AF.Identity,
                                        bias=b_sb[:, oc:oc + 1])
                ins.then_inc(sm.s_ev2, 4)

        @Bl.vector
        def _(v):
            for tt in range(NTILE):
                pr = tt % 2
                v.wait_ge(sm.s_t2, 32 + tt + 1)
                if tt >= 2:
                    v.wait_ge(sm.s_vo, 16 * (tt - 1))
                for h in range(2):
                    ins = v.tensor_tensor(vstage[pr][:, h * 512:(h + 1) * 512],
                                          PB[pr * 2 + h][:], bv_bc[:, h * 512:(h + 1) * 512], ALU.add)
                ins.then_inc(sm.s_ev2, 2)

        @Bl.sync
        def _(sy):
            qo = qh.rearrange("dc p (g t) -> dc g p t", g=2)
            for u in range(16):
                sy.wait_ge(sm.s_ev2, 4 * (u + 1))
                sy.dma_start(qo[u // 2, u % 2], qstage[u % 2][:]).then_inc(sm.s_qo, 16)
            vo = vh.rearrange("(tt p) d -> tt p d", p=128)
            for tt in range(NTILE):
                pr = tt % 2
                sy.wait_ge(sm.s_ev2, 128 + 2 * (tt + 1))
                sy.dma_start(vo[tt], vstage[pr][:]).then_inc(sm.s_vo, 16)

    if phases < 3:
        emit_dbg()
        return nc

    PH3_LVL = int(os.environ.get("PH3_LVL", "4"))
    PH3_NOEVAC = bool(int(os.environ.get("PH3_NOEVAC", "0")))
    with nc.Block("ph3") as Bl:

        @Bl.gpsimd
        def _(g):
            g.wait_ge(sm.s_qo, 16 * 16)
            g.wait_ge(sm.s_vo, 16 * 32)
            g.dma_start(wout_bf[:].rearrange("p (dc j) -> p dc j", dc=DC),
                        wout_d.rearrange("(dc p) j -> p dc j", p=128)).then_inc(sm.s_wo, 16)
            g.dma_start(w1_buf[:].rearrange("p (dc j) -> p dc j", dc=DC),
                        w1a_d.rearrange("e (dc p) j -> e p dc j", p=128)[0]).then_inc(sm.s_w1, 16)

        @Bl.sync
        def _(sy):
            qi = qh.rearrange("dc p (qb t) -> qb p dc t", qb=NQB)
            ao = ah.rearrange("(qb qq p) d -> qb p qq d", p=128, qq=2)
            store_on = PH3_LVL >= 4 and not PH3_NOEVAC

            def qf_load(qb):
                if qb == 0:
                    sy.wait_ge(sm.s_qo, 16 * 16)
                else:
                    sy.wait_ge(sm.s_sc, 32 * qb)
                sy.dma_start(qf_sb[:].rearrange("p (dc t) -> p dc t", dc=DC),
                             qi[qb]).then_inc(sm.s_qf, 16)

            qf_load(0)
            sy.wait_ge(sm.s_xnfo, 16 * NSLAB)
            vi = vh.rearrange("(g tt p) d -> g p tt d", p=128, tt=8)
            vs = v_sb[:].rearrange("p (g tt d) -> p g tt d", g=4, tt=8)
            for gg in range(4):
                sy.dma_start(vs[:, gg], vi[gg]).then_inc(sm.s_vi, 16)
            for qb in range(NQB):
                if qb + 1 < NQB:
                    qf_load(qb + 1)
                if store_on:
                    sy.wait_ge(sm.s_ave, 4 * (qb + 1))
                    sy.dma_start(ao[qb],
                                 attn_st[:].rearrange("p (qq d) -> p qq d", qq=2)).then_inc(sm.s_ao, 16)

        @Bl.tensor
        def _(t):
            def s_reduce(qb):
                t.wait_ge(sm.s_spp, 32 * (qb + 1))
                if qb >= 1:
                    t.wait_ge(sm.s_siv, qb)
                for qq in range(2):
                    ins = t.matmul(PB[6][:, qq:qq + 1], spp[:, qq * 128:(qq + 1) * 128],
                                   onec_sb[:], start=True, stop=True)
                ins.then_inc(sm.s_ts3, 1)

            def outproj(qb):
                t.wait_ge(sm.s_cve, qb + 1)
                if qb == 0:
                    t.wait_ge(sm.s_wo, 16)
                for qq in range(2):
                    for dc in range(DC):
                        lhs = ctx_sb[:, dc * 256 + qq * 128: dc * 256 + (qq + 1) * 128]
                        t.ldweights(lhs)
                        for h in range(2):
                            ins = t.matmul(PB[2 + qq * 2 + h][:], lhs,
                                           wout_bf[:, dc * 1024 + h * 512: dc * 1024 + (h + 1) * 512],
                                           start=(dc == 0), stop=(dc == DC - 1))
                    ins.then_inc(sm.s_to, 1)

            for qb in range(NQB):
                t.wait_ge(sm.s_qf, 16 * (qb + 1))
                for kc in range(KC):
                    gi = qb * KC + kc
                    if gi >= 2:
                        t.wait_ge(sm.s_exp, gi - 1)
                    slot = kc % 4
                    pbs = PB[slot % 2][:, (slot // 2) * 256:(slot // 2 + 1) * 256]
                    for dc in range(DC):
                        lhs = k_sb[:, dc * 4096 + kc * 128: dc * 4096 + (kc + 1) * 128]
                        t.ldweights(lhs)
                        ins = t.matmul(pbs, lhs, qf_sb[:, dc * QBS:(dc + 1) * QBS],
                                       start=(dc == 0), stop=(dc == DC - 1))
                    ins.then_inc(sm.s_sc, 1)
                    if qb >= 1 and PH3_LVL >= 3 and kc == 4:
                        s_reduce(qb - 1)
                    if qb >= 1 and PH3_LVL >= 4 and kc == 10:
                        outproj(qb - 1)
                if PH3_LVL < 2:
                    continue
                t.wait_ge(sm.s_exp, 32 * (qb + 1))
                if qb == 0:
                    t.wait_ge(sm.s_vi, 64)
                elif PH3_LVL >= 4 and not PH3_NOEVAC:
                    t.wait_ge(sm.s_ave, 4 * qb)
                else:
                    t.wait_ge(sm.s_cve, qb)
                for dc in range(DC):
                    pbc = PB[2 + dc // 2][:, (dc % 2) * 256:(dc % 2 + 1) * 256]
                    for kc in range(KC):
                        lhs = v_sb[:, kc * 1024 + dc * 128: kc * 1024 + (dc + 1) * 128]
                        t.ldweights(lhs)
                        ins = t.matmul(pbc, lhs, probs[:, kc * QBS:(kc + 1) * QBS],
                                       start=(kc == 0), stop=(kc == KC - 1))
                    ins.then_inc(sm.s_cx, 1)
            if PH3_LVL >= 3:
                s_reduce(NQB - 1)
            if PH3_LVL >= 4:
                outproj(NQB - 1)

        @Bl.scalar
        def _(sc):
            for qb in range(NQB):
                for kc in range(KC):
                    gi = qb * KC + kc
                    sc.wait_ge(sm.s_sc, gi + 1)
                    if qb >= 1 and kc == 0:
                        if PH3_LVL >= 2:
                            sc.wait_ge(sm.s_cx, 8 * qb)
                        if PH3_LVL >= 1:
                            sc.wait_ge(sm.s_spp, 32 * qb)
                    slot = kc % 4
                    pbs = PB[slot % 2][:, (slot // 2) * 256:(slot // 2 + 1) * 256]
                    sc.activation(probs[:, kc * QBS:(kc + 1) * QBS], pbs, AF.Exp,
                                  scale=1.0 / 32.0).then_inc(sm.s_exp, 1)

        @Bl.vector
        def _(v):
            for qb in range(NQB):
                for kc in range(KC):
                    gi = qb * KC + kc
                    v.wait_ge(sm.s_exp, gi + 1)
                    if kc == 0:
                        ins = v.tensor_copy(spp[:], probs[:, 0:QBS])
                    else:
                        ins = v.tensor_add(spp[:], spp[:], probs[:, kc * QBS:(kc + 1) * QBS])
                    ins.then_inc(sm.s_spp, 1)
                if PH3_LVL >= 3:
                    v.wait_ge(sm.s_ts3, qb + 1)
                    v.reciprocal(s_inv[:, 0:2], PB[6][:, 0:2])
                    v.drain().then_inc(sm.s_siv, 1)
                if PH3_LVL >= 2:
                    v.wait_ge(sm.s_cx, 8 * (qb + 1))
                    for b in range(4):
                        ins = v.tensor_copy(ctx_sb[:, b * 512:(b + 1) * 512], PB[2 + b][:])
                    ins.then_inc(sm.s_cve, 1)
                if PH3_LVL < 4 or PH3_NOEVAC:
                    continue
                if qb >= 1:
                    v.wait_ge(sm.s_ao, 16 * qb)
                for qq in range(2):
                    v.wait_ge(sm.s_to, qb * 2 + qq + 1)
                    for h in range(2):
                        if os.environ.get("PH3_EVAC", "stt") == "copy":
                            ins = v.tensor_copy(
                                attn_st[:, qq * 1024 + h * 512: qq * 1024 + (h + 1) * 512],
                                PB[2 + qq * 2 + h][:])
                        else:
                            ins = v.scalar_tensor_tensor(
                                attn_st[:, qq * 1024 + h * 512: qq * 1024 + (h + 1) * 512],
                                PB[2 + qq * 2 + h][:], s_inv[:, qq:qq + 1],
                                bo_bc[:, h * 512:(h + 1) * 512], ALU.mult, ALU.add)
                    ins.then_inc(sm.s_ave, 2)

    if phases < 4:
        emit_dbg()
        return nc

    def m1_bank_wait(t, idx, gidx, tbg):
        if idx == 0 and gidx == 0:
            return
        if gidx == 0:
            t.wait_ge(sm.s_ge, (idx - 1) * 12 + 10 + tbg + 1)
        elif gidx % 2 == 0:
            t.wait_ge(sm.s_ge, idx * 12 + gidx - 2 + tbg + 1)
        else:
            t.wait_ge(sm.s_ye, idx * 12 + gidx - 1 + tbg + 1)

    with nc.Block("ph4") as Bl:

        @Bl.gpsimd
        def _(g):
            w1v = w1a_d.rearrange("e (dc p) j -> e p dc j", p=128)
            w2v = w2a_d.rearrange("e (hc p) j -> e p hc j", p=128)
            g.dma_start(w2_buf[:].rearrange("p (hc j) -> p hc j", hc=HC2),
                        w2v[0]).then_inc(sm.s_w2, 16)
            for idx in range(1, 2 * NE):
                e = idx % NE
                g.wait_ge(sm.s_m1f, idx)
                g.dma_start(w1_buf[:].rearrange("p (dc j) -> p dc j", dc=DC),
                            w1v[e]).then_inc(sm.s_w1, 16)
                g.wait_ge(sm.s_m2f, idx)
                g.dma_start(w2_buf[:].rearrange("p (hc j) -> p hc j", hc=HC2),
                            w2v[e]).then_inc(sm.s_w2, 16)

        @Bl.sync
        def _(sy):
            xs = x_d.rearrange("(tt p) d -> tt p d", p=128)
            ai = ah.rearrange("(tt p) d -> tt p d", p=128)
            oo = out_d.rearrange("(tt p) d -> tt p d", p=128)
            xf = xfh.rearrange("dc p (hf t) -> hf p dc t", hf=2)
            sy.wait_ge(sm.s_ao, 16 * NQB)

            def prefetch(gt):
                if gt >= 2:
                    sy.wait_ge(sm.s_cmb, gt - 1)
                sy.dma_start(x_re[gt % 2][:], xs[gt]).then_inc(sm.s_x2, 16)
                sy.dma_start(attn_re[gt % 2][:], ai[gt]).then_inc(sm.s_ai, 16)

            for half in range(2):
                if half == 0:
                    sy.dma_start(xh_sb[:].rearrange("p (dc t) -> p dc t", dc=DC),
                                 xf[0]).then_inc(sm.s_xh, 16)
                prefetch(half * TTH)
                prefetch(half * TTH + 1)
                if half == 0:
                    sy.wait_ge(sm.s_m1f, NE)
                    sy.dma_start(xh_sb[:].rearrange("p (dc t) -> p dc t", dc=DC),
                                 xf[1]).then_inc(sm.s_xh, 16)
                for tt in range(TTH):
                    gt = half * TTH + tt
                    sy.wait_ge(sm.s_cmb, gt + 1)
                    sy.dma_start(oo[gt], out_st[gt % 2][:]).then_inc(sm.s_out, 16)
                    if tt + 2 < TTH:
                        prefetch(gt + 2)
            sy.wait_ge(sm.s_out, 16 * 32)

        @Bl.tensor
        def _(t):
            for half in range(2):
                for e in range(NE):
                    idx = half * NE + e
                    t.wait_ge(sm.s_w1, 16 * (idx + 1))
                    t.wait_ge(sm.s_xh, 16 * (half + 1))
                    for gidx, hc in enumerate(HCORD):
                        for tbg in range(2):
                            gg = gidx * 2 + tbg
                            m1_bank_wait(t, idx, gidx, tbg)
                            for dc in range(DC):
                                lhs = w1_buf[:, dc * 1536 + hc * 128: dc * 1536 + (hc + 1) * 128]
                                t.ldweights(lhs)
                                for tb in range(2):
                                    ins = t.matmul(PB[tbg * 2 + tb][:], lhs,
                                                   xh_sb[:, dc * 2048 + tbg * 1024 + tb * 512:
                                                         dc * 2048 + tbg * 1024 + (tb + 1) * 512],
                                                   start=(dc == 0), stop=(dc == DC - 1))
                            if gg == 23:
                                ins.then_inc(sm.s_m1g, 1)
                                t.nop().then_inc(sm.s_m1f, 1)
                            else:
                                ins.then_inc(sm.s_m1g, 1)
                    t.wait_ge(sm.s_w2, 16 * (idx + 1))
                    t.wait_ge(sm.s_act, idx * HC2 + HC2 - 1)
                    for tt in range(TTH):
                        zi = idx * TTH + tt
                        if zi >= 2:
                            t.wait_ge(sm.s_zadd, zi - 1)
                        if tt == 1:
                            t.wait_ge(sm.s_act, idx * HC2 + HC2)
                        for hc in range(HC2):
                            if tt == 0 and hc == HC2 - 1:
                                t.wait_ge(sm.s_act, idx * HC2 + HC2)
                            lhs = y_sb[:, hc * 2048 + tt * 128: hc * 2048 + (tt + 1) * 128]
                            t.ldweights(lhs)
                            for h in range(2):
                                ins = t.matmul(PB[4 + (tt % 2) * 2 + h][:], lhs,
                                               w2_buf[:, hc * 1024 + h * 512: hc * 1024 + (h + 1) * 512],
                                               start=(hc == 0), stop=(hc == HC2 - 1))
                        if tt == TTH - 1:
                            ins.then_inc(sm.s_m2t, 1)
                            t.nop().then_inc(sm.s_m2f, 1)
                        else:
                            ins.then_inc(sm.s_m2t, 1)

        @Bl.scalar
        def _(sc):
            for half in range(2):
                for e in range(NE):
                    idx = half * NE + e
                    for gidx, hc in enumerate(HCORD):
                        if hc < HC2:
                            continue
                        for tbg in range(2):
                            gg = gidx * 2 + tbg
                            sc.wait_ge(sm.s_m1g, idx * 24 + gg + 1)
                            for tb in range(2):
                                ins = sc.activation(
                                    sg_sb[:, (hc - HC2) * 2048 + tbg * 1024 + tb * 512:
                                          (hc - HC2) * 2048 + tbg * 1024 + (tb + 1) * 512],
                                    PB[tbg * 2 + tb][:], AF.Silu)
                            ins.then_inc(sm.s_ge, 1)

        @Bl.vector
        def _(v):
            for half in range(2):
                for e in range(NE):
                    idx = half * NE + e
                    for gidx, hc in enumerate(HCORD):
                        if hc >= HC2:
                            continue
                        for tbg in range(2):
                            gg = gidx * 2 + tbg
                            v.wait_ge(sm.s_m1g, idx * 24 + gg + 1)
                            for tb in range(2):
                                ins = v.tensor_copy(
                                    y_sb[:, hc * 2048 + tbg * 1024 + tb * 512:
                                         hc * 2048 + tbg * 1024 + (tb + 1) * 512],
                                    PB[tbg * 2 + tb][:])
                            ins.then_inc(sm.s_ye, 1)
                        v.wait_ge(sm.s_ge, idx * 12 + (hc + 1) * 2)
                        v.tensor_mul(y_sb[:, hc * 2048:(hc + 1) * 2048],
                                     y_sb[:, hc * 2048:(hc + 1) * 2048],
                                     sg_sb[:, hc * 2048:(hc + 1) * 2048]).then_inc(sm.s_act, 1)
                    for tt in range(TTH):
                        zi = idx * TTH + tt
                        gt = half * TTH + tt
                        v.wait_ge(sm.s_m2t, zi + 1)
                        wcol = w_all[:, gt * 8 + min(e, 7): gt * 8 + min(e, 7) + 1]
                        for h in range(2):
                            dst = z_sb[:, tt * 1024 + h * 512: tt * 1024 + (h + 1) * 512]
                            pbm = PB[4 + (tt % 2) * 2 + h][:]
                            if e == 0:
                                ins = v.tensor_scalar(dst, pbm, wcol, None, ALU.mult)
                            elif e < 8:
                                ins = v.scalar_tensor_tensor(dst, pbm, wcol, dst, ALU.mult, ALU.add)
                            else:
                                ins = v.tensor_tensor(dst, pbm, dst, ALU.add)
                        ins.then_inc(sm.s_zadd, 1)
                        if e == NE - 1:
                            v.wait_ge(sm.s_x2, 16 * (gt + 1))
                            v.wait_ge(sm.s_ai, 16 * (gt + 1))
                            if gt > 1:
                                v.wait_ge(sm.s_out, 16 * (gt - 1))
                            ob = out_st[gt % 2]
                            v.tensor_add(ob[:], x_re[gt % 2][:], z_sb[:, tt * 1024:(tt + 1) * 1024])
                            v.tensor_add(ob[:], ob[:], attn_re[gt % 2][:]).then_inc(sm.s_cmb, 1)

    emit_dbg()
    return nc


_NC_CACHE = {}


def _get_nc(phases=4, debug=False):
    key = (phases, debug)
    if key not in _NC_CACHE:
        _NC_CACHE[key] = build(phases, debug)
    return _NC_CACHE[key]


def prep_inputs(L0_attn_x, L0_attn_W_qkv, L0_attn_b_qkv, L0_attn_W_out, L0_attn_b_out,
                L0_ffn_gate_w, L0_ffn_W1, L0_ffn_W2, L0_ffn_Ws1, L0_ffn_Ws2):
    f = np.float32
    x = np.ascontiguousarray(np.asarray(L0_attn_x, dtype=f))
    wqkv = np.ascontiguousarray(np.asarray(L0_attn_W_qkv, dtype=f))
    bqkv = np.asarray(L0_attn_b_qkv, dtype=f)
    bqk = np.ascontiguousarray(bqkv[:2048].reshape(16, 128).T)
    bv = np.ascontiguousarray(bqkv[2048:].reshape(1, 1024))
    wout = np.ascontiguousarray(np.asarray(L0_attn_W_out, dtype=f))
    bo = np.ascontiguousarray(np.asarray(L0_attn_b_out, dtype=f).reshape(1, 1024))
    gwt = np.ascontiguousarray(np.asarray(L0_ffn_gate_w, dtype=f).T)
    W1 = np.asarray(L0_ffn_W1, dtype=f)
    W2 = np.asarray(L0_ffn_W2, dtype=f)
    Ws1 = np.asarray(L0_ffn_Ws1, dtype=f)
    Ws2 = np.asarray(L0_ffn_Ws2, dtype=f)

    w1a = np.empty((NE, D, 2 * HEXP), dtype=f)
    w2a = np.empty((NE, HEXP, D), dtype=f)
    for e in range(8):
        w1a[e] = W1[e].T
        w2a[e] = W2[e].T
    ws1t = Ws1.T
    ws2t = Ws2.T
    w1a[8] = np.concatenate([ws1t[:, 0:768], ws1t[:, 1536:2304]], axis=1)
    w1a[9] = np.concatenate([ws1t[:, 768:1536], ws1t[:, 2304:3072]], axis=1)
    w2a[8] = ws2t[0:768]
    w2a[9] = ws2t[768:1536]
    w1a = np.ascontiguousarray(w1a)
    w2a = np.ascontiguousarray(w2a)

    shared = {"wqkv": wqkv, "bqk": bqk, "bv": bv, "bo": bo, "wout": wout,
              "gwt": gwt, "w1a": w1a, "w2a": w2a}
    in_maps = []
    for c in range(8):
        m = {"x": np.ascontiguousarray(x[c])}
        m.update(shared)
        in_maps.append(m)
    return in_maps


def kernel(**inputs):
    in_maps = prep_inputs(**inputs)
    nc = _get_nc()
    res = run_bass_kernel_spmd(nc, in_maps, core_ids=list(range(8)))
    return np.stack([r["out"] for r in res.results], axis=0)
